# version 27
# baseline (speedup 1.0000x reference)
import contextlib
import os
import numpy as np
import ml_dtypes
import concourse.bass as bass
import concourse.mybir as mybir
from concourse.bass_utils import run_bass_kernel_spmd

F32 = mybir.dt.float32
BF16 = mybir.dt.bfloat16
AF = mybir.ActivationFunctionType
ALU = mybir.AluOpType
AX = mybir.AxisListType

DM, DI, NST = 1024, 2048, 16
NKC, NIC = DM // 128, DI // 128
TC, TBLK = 512, 4
LN_EPS = 1e-5
N_CORES = 8


class Sem:
    def __init__(self, handle, step=1):
        self.h, self.n, self.step = handle, 0, step


class Builder:
    ENGINES = ("sync", "gpsimd", "vector", "scalar", "tensor")

    def __init__(self):
        self.ops = {e: [] for e in self.ENGINES}

    def dma(self, eng, dst_ap, src_ap, sem, waits=()):
        sem.n += 16
        tgt = sem.n
        w = tuple((s.h, v) for (s, v) in waits if v)

        def f(e, dst_ap=dst_ap, src_ap=src_ap, w=w, sem=sem):
            for (h, v) in w:
                e.wait_ge(h, v)
            e.dma_start(out=dst_ap, in_=src_ap).then_inc(sem.h, 16)
        self.ops[eng].append(f)
        return tgt

    def op(self, eng, fn, sem=None, waits=(), drain=False):
        if sem is not None:
            sem.n += 1
        tgt = sem.n if sem is not None else None
        w = tuple((s.h, v) for (s, v) in waits if v)
        assert not (drain and sem is None)

        def f(e, fn=fn, w=w, sem=sem, tgt=tgt, drain=drain):
            for (h, v) in w:
                e.wait_ge(h, v)
            ins = fn(e)
            if sem is not None:
                ins.then_inc(sem.h, 1)
            if drain:
                e.wait_ge(sem.h, tgt)
        self.ops[eng].append(f)
        return tgt

    def wait(self, eng, sem, val):
        if val:
            def f(e, sem=sem, val=val):
                e.wait_ge(sem.h, val)
            self.ops[eng].append(f)


def build_program(T):
    NCH = T // TC
    nc = bass.Bass("TRN2", target_bir_lowering=False, debug=False,
                   detect_race_conditions=False)

    def din(name, shape, dtype=F32):
        return nc.dram_tensor(name, shape, dtype, kind="ExternalInput").ap()

    x_hbm = din("x", [T, DM])
    winT = din("winT", [DM, 2 * DI])
    woutT = din("woutT", [DI, DM])
    baseT = din("baseT", [DM, DM])
    wxT = din("wxT", [DI, 33])
    loraB = din("loraB", [64, DM])
    rlT = din("rlT", [DM, 16])
    ln_g = din("ln_g", [DM])
    ln_b = din("ln_b", [DM])
    conv_w = din("conv_w", [DI, 4])
    conv_b = din("conv_b", [DI])
    w_dt = din("w_dt", [DI, 1])
    b_dt = din("b_dt", [DI])
    a_log = din("a_log", [DI, NST])
    d_prm = din("d_prm", [DI])
    base_b = din("base_b", [DM])
    rout_b = din("rout_b", [16, 1])
    ident = din("ident", [128, 128])
    identb = din("identb", [128, 128], BF16)
    ones1 = din("ones1", [96, 128])
    cvals = din("cvals", [128, 4])

    out_hbm = nc.dram_tensor("out", [T, DM], F32, kind="ExternalOutput").ap()
    dbg_kind = "ExternalOutput" if os.environ.get("KDBG") else "Internal"
    win_sc = nc.dram_tensor("win_sc", [DM, 2 * DI], BF16, kind="Internal").ap()
    wout_sc = nc.dram_tensor("wout_sc", [DI, DM], BF16, kind="Internal").ap()
    base_sc = nc.dram_tensor("base_sc", [DM, DM], BF16, kind="Internal").ap()
    sp_xc = nc.dram_tensor("sp_xc", [NCH * 8 * 128, TC], F32, kind=dbg_kind).ap()
    dbg_on = dbg_kind == "ExternalOutput"
    xn_sc2 = nc.dram_tensor("xn_sc2", [2, NKC * 128, TC], BF16,
                            kind="Internal").ap()
    if dbg_on:
        dbg_xnf = nc.dram_tensor("dbg_xnf", [NCH * 128, NKC * TC], BF16,
                                 kind="ExternalOutput").ap()
        dbg_xnnf = nc.dram_tensor("dbg_xnnf", [NCH * 128, NKC * TC], BF16,
                                  kind="ExternalOutput").ap()
        dbg_y3 = nc.dram_tensor("dbg_y3", [NCH * 128, NIC * TC], BF16,
                                kind="ExternalOutput").ap()
    sp_sz = nc.dram_tensor("sp_sz", [NCH * NIC * 128, TC], F32, kind=dbg_kind).ap()
    sp_g = nc.dram_tensor("sp_g", [NCH * NKC * 128, TC], F32, kind=dbg_kind).ap()

    with contextlib.ExitStack() as stk:
        En = stk.enter_context

        def sbuf(name, shape, dtype=F32):
            return En(nc.sbuf_tensor(name, shape, dtype))

        x_T = sbuf("x_T", [128, TBLK * DM])
        xn_T = sbuf("xn_T", [128, DM])
        xn_f = sbuf("xn_f", [128, NKC * TC], BF16)
        xnn_f = sbuf("xnn_f", [128, NKC * TC], BF16)
        scr1 = sbuf("scr1", [128, DM])
        xc_h = sbuf("xc_h", [128, 8 * TC])
        dt_h = sbuf("dt_h", [128, 8 * TC])
        u_h = sbuf("u_h", [128, 8 * TC], BF16)
        QW = 4 * (TC + 1)
        abar = sbuf("abar", [128, 3 * QW], BF16)
        beta = sbuf("beta", [128, 2 * QW], BF16)
        h_q = sbuf("h_q", [128, 4 * QW], BF16)
        carry = sbuf("carry", [128, NIC * NST])
        y_Th = sbuf("y_Th", [128, TBLK * 8 * 128])
        y3 = sbuf("y3", [128, NIC * TC], BF16)
        ssm = sbuf("ssm", [128, TC])
        ssm_T = sbuf("ssm_T", [128, TBLK * 33])
        win_b = sbuf("win_b", [128, 4 * NKC * 128], BF16)
        wob_b = sbuf("wob_b", [128, 2 * NIC * 128], BF16)
        wx_r = sbuf("wx_r", [128, NIC * 33], BF16)
        lb_r = sbuf("lb_r", [64, DM], BF16)
        rl_r = sbuf("rl_r", [128, NKC * 16], BF16)
        g_t = sbuf("g_t", [128, 2 * TC])
        sz_t = sbuf("sz_t", [128, 2 * TC])
        tp = sbuf("tp", [128, 2 * TC])
        xcb = sbuf("xcb", [128, 2 * TC], BF16)
        lgr = sbuf("lgr", [128, TC])
        lm_T = sbuf("lm_T", [128, TBLK * 16])
        sfx = sbuf("sfx", [128, TBLK * 12])
        gat_T = sbuf("gat_T", [128, TBLK * 8])
        gat = sbuf("gat", [8, TC], BF16)
        grep = sbuf("grep", [64, TC])
        mrep = sbuf("mrep", [64, TC])
        midg = sbuf("midg", [64, TC], BF16)
        stat = sbuf("stat", [128, 32])
        idn = sbuf("idn", [128, 128])
        idnb = sbuf("idnb", [128, 128], BF16)
        x_rt = sbuf("x_rt", [128, 2 * TC])
        onesr = sbuf("onesr", [96, 128])
        gam_c = sbuf("gam_c", [128, NKC])
        bet_c = sbuf("bet_c", [128, NKC])
        cw_c = sbuf("cw_c", [128, NIC * 4])
        cb_c = sbuf("cb_c", [128, NIC])
        cb2_c = sbuf("cb2_c", [128, NIC])
        wsum_c = sbuf("wsum_c", [128, NIC])
        wdt_c = sbuf("wdt_c", [128, NIC])
        bdt_c = sbuf("bdt_c", [128, NIC])
        a_c = sbuf("a_c", [128, NIC * NST])
        d_c = sbuf("d_c", [128, NIC])
        bb_c = sbuf("bb_c", [128, NKC])
        bxz_c = sbuf("bxz_c", [128, 2 * NIC])
        rb_c = sbuf("rb_c", [16, 1])
        cv_s = sbuf("cv_s", [128, 4])
        bc_st = sbuf("bc_st", [96, TC])
        b_rep = sbuf("b_rep", [128, 3 * TC], BF16)
        ccar = sbuf("ccar", [128, NIC * 4])
        stg = dt_h
        stgb = u_h.ap()
        e1t = scr1[:, 0:TC]
        scr1b = scr1.ap().bitcast(BF16)
        ostage = g_t
        tp2 = scr1[:, TC:2 * TC]

        ps = [En(nc.psum_tensor(f"ps{i}", [128, 512], F32)) for i in range(8)]

        s_ldx = Sem(En(nc.semaphore("s_ldx")), 16)
        s_xnw = Sem(En(nc.semaphore("s_xnw")), 16)
        s_xnl = [Sem(En(nc.semaphore(f"s_xnl{i}")), 16) for i in range(2)]
        s_ldw = Sem(En(nc.semaphore("s_ldw")), 16)
        s_sp = Sem(En(nc.semaphore("s_sp")), 16)
        s_bsp = Sem(En(nc.semaphore("s_bsp")), 16)
        s_out = Sem(En(nc.semaphore("s_out")), 16)
        s_win = [Sem(En(nc.semaphore(f"s_win{i}")), 16) for i in range(4)]
        s_wob = [Sem(En(nc.semaphore(f"s_wob{i}")), 16) for i in range(2)]
        s_gl = [Sem(En(nc.semaphore(f"s_gl{i}")), 16) for i in range(2)]
        s_szl = [Sem(En(nc.semaphore(f"s_szl{i}")), 16) for i in range(2)]
        s_xcl = Sem(En(nc.semaphore("s_xcl")), 16)
        s_bc = [Sem(En(nc.semaphore(f"s_bc{i}")), 16) for i in range(3)]
        s_rep = Sem(En(nc.semaphore("s_rep")), 16)
        s_xrt = [Sem(En(nc.semaphore(f"s_xrt{i}")), 16) for i in range(2)]
        s_ost = [Sem(En(nc.semaphore(f"s_ost{i}")), 16) for i in range(2)]
        s_T = Sem(En(nc.semaphore("s_T")))
        s_V = Sem(En(nc.semaphore("s_V")))
        s_S = Sem(En(nc.semaphore("s_S")))
        block = En(nc.Block())

        B = Builder()
        dma, op = B.dma, B.op
        free = {i: (s_V, 0) for i in range(8)}
        nc.const_aps.register(F32, LN_EPS, cv_s[:, 0:1])
        nc.const_aps.register(F32, -0.5, cv_s[:, 1:2])
        nc.const_aps.register(F32, -1.0, cv_s[:, 2:3])
        nc.const_aps.register(F32, 1.0, cv_s[:, 3:4])

        dma("sync", gam_c.ap(), ln_g.rearrange("(c p) -> p c", p=128), s_ldw)
        t_bet = dma("sync", bet_c.ap(),
                    ln_b.rearrange("(c p) -> p c", p=128), s_ldw)
        dma("sync", cw_c.ap().rearrange("p (c k) -> p c k", k=4),
            conv_w.rearrange("(c p) k -> p c k", p=128), s_ldw)
        dma("sync", cb_c.ap(), conv_b.rearrange("(c p) -> p c", p=128), s_ldw)
        dma("sync", wdt_c.ap().unsqueeze(2),
            w_dt.rearrange("(c p) o -> p c o", p=128), s_ldw)
        dma("sync", bdt_c.ap(), b_dt.rearrange("(c p) -> p c", p=128), s_ldw)
        t_al = dma("sync", a_c.ap().rearrange("p (c n) -> p c n", n=NST),
                   a_log.rearrange("(c p) n -> p c n", p=128), s_ldw)
        dma("sync", d_c.ap(), d_prm.rearrange("(c p) -> p c", p=128), s_ldw)
        dma("sync", bb_c.ap(), base_b.rearrange("(c p) -> p c", p=128), s_ldw)
        t_rb = dma("sync", rb_c[0:16, :], rout_b, s_ldw)
        t_id = dma("sync", idn.ap(), ident, s_ldw)
        dma("sync", idnb.ap(), identb, s_ldw)
        dma("sync", cv_s.ap(), cvals, s_ldw)
        t_on = dma("sync", onesr.ap(), ones1, s_ldw)
        t_wx = dma("sync",
                   stg[:, 0:NIC * 33].rearrange("p (c m) -> p c m", m=33),
                   wxT.rearrange("(c p) m -> p c m", p=128), s_ldw)
        t_lb = dma("sync", xc_h[0:64, 0:DM], loraB, s_ldw)
        t_rl = dma("sync",
                   stg[:, NIC * 33:NIC * 33 + NKC * 16].rearrange(
                       "p (c m) -> p c m", m=16),
                   rlT.rearrange("(c p) m -> p c m", p=128), s_ldw)
        t_small = s_ldw.n

        op("vector", lambda e: e.memset(ssm[:], 0.0), s_V)
        op("vector", lambda e: e.memset(lgr[:], 0.0), s_V)
        op("vector", lambda e: e.memset(carry[:], 0.0), s_V)
        op("vector", lambda e: e.memset(abar[:], 0.0), s_V)
        op("scalar", lambda e: e.activation(a_c[:], a_c[:], AF.Exp),
           s_S, waits=[(s_ldw, t_small)], drain=True)
        op("scalar", lambda e: e.mul(a_c[:], a_c[:], -1.0), s_S)
        tV_wx = op("vector", lambda e: e.tensor_copy(wx_r[:], stg[:, 0:NIC * 33]),
                   s_V, waits=[(s_ldw, t_small)])
        tV_lb = op("vector", lambda e: e.tensor_scalar(
            lb_r[:], xc_h[0:64, 0:DM], 2.0, None, ALU.mult), s_V,
            waits=[(s_ldw, t_small)])
        tV_rl = op("vector", lambda e: e.tensor_copy(
            rl_r[:], stg[:, NIC * 33:NIC * 33 + NKC * 16]), s_V,
            waits=[(s_ldw, t_small)])

        last_sp = 0
        tSb = None
        B.wait("sync", s_ldw, t_small)
        for pc in range(8):
            cols = slice(pc * 512, (pc + 1) * 512)
            t_pc = dma("sync",
                       stg.ap().rearrange("p (c m) -> p c m", m=512),
                       winT[:, cols].rearrange("(c p) m -> p c m", p=128),
                       s_ldw, waits=[(s_sp, last_sp),
                                     (s_V, tV_rl if pc == 0 else 0),
                                     (s_V, tV_wx if pc == 0 else 0)])
            tV = None
            for kc in range(NKC):
                tV = op("vector", lambda e, kc=kc: e.tensor_scalar(
                    stgb[:, kc * 512:(kc + 1) * 512],
                    stg[:, kc * 512:(kc + 1) * 512],
                    gam_c[:, kc:kc + 1], None, ALU.mult), s_V,
                    waits=[(s_ldw, t_pc)])
            for m2 in range(4):
                tT = None
                for kc in range(NKC):
                    tT = op("tensor", lambda e, kc=kc, m2=m2: e.matmul(
                        ps[7][:, 0:1],
                        stg[:, kc * 512 + m2 * 128:kc * 512 + m2 * 128 + 128],
                        bet_c[:, kc:kc + 1],
                        start=(kc == 0), stop=(kc == NKC - 1)), s_T,
                        waits=([(s_ldw, t_pc), (s_ldw, t_small), free[7]]
                               if kc == 0 else []))
                tSb = op("scalar", lambda e, pc=pc, m2=m2: e.copy(
                    bxz_c[:, pc * 4 + m2:pc * 4 + m2 + 1], ps[7][:, 0:1]),
                    s_S, waits=[(s_T, tT)])
                free[7] = (s_S, tSb)
            last_sp = dma("gpsimd",
                          win_sc[:, cols].rearrange("(c p) m -> p c m", p=128),
                          stgb[:, 0:NKC * 512].rearrange(
                              "p (c m) -> p c m", m=512), s_sp,
                          waits=[(s_V, tV), (s_S, tSb)])
        t_bxz = tSb

        for (src, dst, nkc) in ((woutT, wout_sc, NIC), (baseT, base_sc, NKC)):
            for pc in range(src.shape[1] // 256):
                c0 = pc * 256
                t_hp = dma("sync",
                           stg[:, 0:nkc * 256].rearrange(
                               "p (c m) -> p c m", m=256),
                           src[:, c0:c0 + 256].rearrange(
                               "(c p) m -> p c m", p=128),
                           s_ldw, waits=[(s_sp, last_sp)])
                tV = op("vector", lambda e, nkc=nkc: e.tensor_copy(
                    stgb[:, 0:nkc * 256], stg[:, 0:nkc * 256]), s_V,
                    waits=[(s_ldw, t_hp)])
                last_sp = dma("gpsimd",
                              dst[:, c0:c0 + 256].rearrange(
                                  "(c p) m -> p c m", p=128),
                              stgb[:, 0:nkc * 256].rearrange(
                                  "p (c m) -> p c m", m=256),
                              s_sp, waits=[(s_V, tV)])

        for i in range(NIC):
            op("vector", lambda e, i=i: e.tensor_reduce(
                wsum_c[:, i:i + 1], cw_c[:, 4 * i:4 * i + 4], AX.X, ALU.add),
                s_V, waits=([(s_ldw, t_small)] if i == 0 else []), drain=True)
        op("vector", lambda e: e.tensor_tensor(
            cb2_c[:], wsum_c[:], bxz_c[:, 0:NIC], ALU.mult), s_V,
            waits=[(s_S, t_bxz)], drain=True)
        op("vector", lambda e: e.tensor_tensor(
            cb2_c[:], cb2_c[:], cb_c[:], ALU.add), s_V)
        for i in range(NIC):
            op("vector", lambda e, i=i: e.tensor_scalar(
                ccar[:, 4 * i:4 * i + 3],
                bxz_c[:, i:i + 1].broadcast_to([128, 3]),
                -1.0, None, ALU.mult), s_V)

        x_t3 = x_hbm.rearrange("(c b p) d -> c b p d", b=TBLK, p=128)
        out_t3 = out_hbm.rearrange("(c b p) d -> c b p d", b=TBLK, p=128)
        sp_xc3 = sp_xc.rearrange("(a p) t -> a p t", p=128)
        sp_sz3 = sp_sz.rearrange("(a p) t -> a p t", p=128)
        sp_g3 = sp_g.rearrange("(a p) t -> a p t", p=128)

        scratch_ready = last_sp
        xT_rd_done = (0, 0)
        xnT_transp = 0
        ln1_srd = 0
        ln1_vrd = 0
        ln2_srd = 0
        ln2_vrd = 0
        xnf_ok = 0
        tS_evA = 0
        xc_free0 = tV_lb
        win_mm = [0, 0, 0, 0]
        wob_mm = [0, 0]
        wx_cons = 0
        ssm_cons = 0
        ssm_free = 0
        ssmT_free = 0
        ymT_done = 0
        y3_free = 0
        xc_free = xc_free0
        dt_free = 0
        gat_free = 0
        tts_hfree = {0: 0, 1: 0, 2: 0, 3: 0}
        carry_hfree = {0: 0, 1: 0, 2: 0, 3: 0}
        ab_free = {0: 0, 1: 0, 2: 0}
        qi_count = 0
        abq_count = 0
        brep_cons = {0: 0, 1: 0, 2: 0}
        tpA_free = 0
        szt_wr = [(s_V, 0), (s_V, 0)]
        gt_wr = [(s_T, 0), (s_T, 0)]
        scr1_freeT = 0
        bc_slot_free = {0: 0, 1: 0, 2: 0}
        midg_rd_prev = 0
        o3_prev = [0, 0]
        lora_mm_prev = 0

        t_x = dma("sync", x_T.ap().rearrange("p (b d) -> p b d", d=DM),
                  x_t3[0].rearrange("b p d -> p b d"), s_ldx)

        def A_tb(c, tb):
            nonlocal xnT_transp, scr1_freeT, xT_rd_done, tS_evA
            nonlocal ln1_srd, ln1_vrd, ln2_srd, ln2_vrd
            if True:
                xs = x_T[:, tb * DM:(tb + 1) * DM]
                xns = xn_T[:, 0:DM]
                w0 = ([(s_ldx, t_x), (s_T, xnT_transp)] +
                      ([(s_ldw, t_small)] if c == 0 else [])) if tb == 0 else []
                op("vector", lambda e, xs=xs: e.bn_stats(
                    stat[:, 0:6], xs[:, 0:512]), s_V, waits=w0)
                op("vector", lambda e, xs=xs: e.bn_stats(
                    stat[:, 6:12], xs[:, 512:1024]), s_V, drain=True)
                tVa = op("vector", lambda e: e.bn_aggr(
                    stat[:, 12:14], stat[:, 0:12]), s_V,
                    waits=[(s_S, ln1_srd)])
                op("scalar", lambda e: e.activation(
                    stat[:, 14:15], stat[:, 13:14], AF.Ln, bias=LN_EPS), s_S,
                    waits=[(s_V, tVa), (s_V, ln1_vrd)], drain=True)
                op("scalar", lambda e: e.activation(
                    stat[:, 14:15], stat[:, 14:15], AF.Exp, scale=-0.5), s_S,
                    drain=True)
                op("scalar", lambda e: e.activation(
                    stat[:, 15:16], stat[:, 12:13], AF.Copy,
                    scale=stat[:, 14:15]), s_S, drain=True)
                tS4 = op("scalar", lambda e: e.activation(
                    stat[:, 15:16], stat[:, 15:16], AF.Copy, scale=-1.0), s_S)
                ln1_srd = tS4
                tVxn = op("vector", lambda e, xs=xs, xns=xns: e.tensor_scalar(
                    xns, xs, stat[:, 14:15], stat[:, 15:16], ALU.mult,
                    ALU.add), s_V, waits=[(s_S, tS4)])
                ln1_vrd = tVxn
                op("vector", lambda e, xns=xns: e.bn_stats(
                    stat[:, 16:22], xns[:, 0:512]), s_V)
                op("vector", lambda e, xns=xns: e.bn_stats(
                    stat[:, 22:28], xns[:, 512:1024]), s_V, drain=True)
                tVb = op("vector", lambda e: e.bn_aggr(
                    stat[:, 28:30], stat[:, 16:28]), s_V,
                    waits=[(s_S, ln2_srd)])
                op("scalar", lambda e: e.activation(
                    stat[:, 30:31], stat[:, 29:30], AF.Ln, bias=LN_EPS), s_S,
                    waits=[(s_V, tVb), (s_V, ln2_vrd)], drain=True)
                op("scalar", lambda e: e.activation(
                    stat[:, 30:31], stat[:, 30:31], AF.Exp, scale=-0.5), s_S,
                    drain=True)
                op("scalar", lambda e: e.activation(
                    stat[:, 31:32], stat[:, 28:29], AF.Copy,
                    scale=stat[:, 30:31]), s_S, drain=True)
                tS8 = op("scalar", lambda e: e.activation(
                    stat[:, 31:32], stat[:, 31:32], AF.Copy, scale=-1.0), s_S)
                ln2_srd = tS8
                tVnn = op("vector", lambda e, xns=xns: e.tensor_scalar(
                    scr1b[:, 0:DM], xns, stat[:, 30:31], stat[:, 31:32],
                    ALU.mult, ALU.add), s_V, waits=[(s_S, tS8)])
                ln2_vrd = tVnn
                tVxb = op("vector", lambda e, xns=xns: e.tensor_scalar(
                    scr1b[:, DM:2 * DM], xns, 1.0, None, ALU.mult), s_V)
                for (off, dstbuf, wv) in (
                        (0, xnn_f, tVnn), (DM, xn_f, tVxb)):
                    for grp in range(2):
                        pb = 2 + grp
                        tT = None
                        for q4 in range(4):
                            dk = grp * 4 + q4
                            tT = op("tensor",
                                    lambda e, off=off, dk=dk,
                                    pb=pb, q4=q4: e.transpose(
                                        ps[pb].ap().bitcast(BF16)[
                                            :, q4 * 128:(q4 + 1) * 128],
                                        scr1b[:, off + dk * 128:
                                              off + dk * 128 + 128],
                                        idnb[:]), s_T,
                                    waits=([(s_V, wv), (s_ldw, t_small), free[pb]]
                                           if q4 == 0 else []))
                        dst = dstbuf.ap().rearrange(
                            "p (k t) -> p k t", t=TC)[
                            :, grp * 4:grp * 4 + 4, tb * 128:(tb + 1) * 128]
                        tS_evA = op("scalar", lambda e, pb=pb, dst=dst: e.copy(
                            dst, ps[pb].ap().bitcast(BF16)[:, 0:512].rearrange(
                                "p (k t) -> p k t", t=128)), s_S,
                            waits=[(s_T, tT)])
                        free[pb] = (s_S, tS_evA)
                scr1_freeT = tT
                xnT_transp = tT
                xT_rd_done = (tVxn, 0)

        def A_tail(c):
            nonlocal xnf_ok
            xnf_ok = tS_evA
            if dbg_on:
                dma("gpsimd", dbg_xnf.rearrange("(c p) m -> c p m", p=128)[c],
                    xn_f.ap(), s_sp, waits=[(s_S, xnf_ok)])
                dma("gpsimd", dbg_xnnf.rearrange("(c p) m -> c p m", p=128)[c],
                    xnn_f.ap(), s_sp, waits=[(s_S, xnf_ok)])
            dma("gpsimd",
                xn_sc2[c % 2].rearrange("(k p) t -> p k t", p=128),
                xn_f.ap().rearrange("p (k t) -> p k t", t=TC), s_xnw,
                waits=[(s_S, xnf_ok)])

        def emit_A_parts(c):
            return [(lambda tb=tb: A_tb(c, tb)) for tb in range(TBLK)] + \
                [lambda: A_tail(c)]

        for p_ in emit_A_parts(0):
            p_()
        for c in range(NCH):
            d_hold = {}
            d_pre = {0: [], 1: [], 2: [], 3: []}
            for dblk in range(4):
                par = dblk % 4
                reg0 = par * NKC * 128
                d_hold[dblk] = dma(
                    "sync" if par % 2 == 0 else "scalar",
                    win_b[:, reg0:reg0 + NKC * 128].rearrange(
                        "p (c m) -> p c m", m=128),
                    win_sc[:, dblk * 128:(dblk + 1) * 128].rearrange(
                        "(c p) m -> p c m", p=128),
                    s_win[par], waits=[(s_T, win_mm[par]),
                                       (s_sp, scratch_ready
                                        if c == 0 else 0)])
            tT = None
            for kc in range(NKC):
                tT = op("tensor", lambda e, kc=kc: e.matmul(
                    ps[0][0:16, 0:TC], rl_r[:, kc * 16:(kc + 1) * 16],
                    xn_f[:, kc * TC:(kc + 1) * TC],
                    start=(kc == 0), stop=(kc == NKC - 1)), s_T,
                    waits=([(s_S, xnf_ok), free[0]] if kc == 0 else []))
            tS_lg = op("scalar", lambda e: e.add(
                lgr[0:16, :], ps[0][0:16, 0:TC], rb_c[0:16, 0:1]), s_S,
                waits=[(s_T, tT)])
            free[0] = (s_S, tS_lg)
            for tb in range(TBLK):
                pb = 2 + tb % 2
                tT = op("tensor", lambda e, tb=tb, pb=pb: e.transpose(
                    ps[pb][:, 0:128], lgr[:, tb * 128:(tb + 1) * 128],
                    idn[:]), s_T, waits=[(s_S, tS_lg), free[pb]])
                tS_ev = op("scalar", lambda e, tb=tb, pb=pb: e.copy(
                    lm_T[:, tb * 16:(tb + 1) * 16], ps[pb][:, 0:16]), s_S,
                    waits=[(s_T, tT)])
                free[pb] = (s_S, tS_ev)
            tV_g = 0
            for tb in range(TBLK):
                o = tb * 12
                lo = tb * 16
                op("vector", lambda e, lo=lo, o=o: e.reduce_max(
                    sfx[:, o + 8:o + 9], lm_T[:, lo:lo + 8], AX.X), s_V,
                    waits=[(s_S, tS_ev)] if tb == 0 else [], drain=True)
                op("vector", lambda e, o=o: e.tensor_scalar(
                    sfx[:, o + 9:o + 10], sfx[:, o + 8:o + 9], -1.0, None,
                    ALU.mult), s_V, drain=True)
                tS_x = op("scalar", lambda e, lo=lo, o=o: e.activation(
                    sfx[:, o:o + 8], lm_T[:, lo:lo + 8], AF.Exp,
                    bias=sfx[:, o + 9:o + 10]), s_S, waits=[(s_V, s_V.n)])
                op("vector", lambda e, o=o: e.tensor_reduce(
                    sfx[:, o + 10:o + 11], sfx[:, o:o + 8], AX.X, ALU.add),
                    s_V, waits=[(s_S, tS_x)], drain=True)
                op("vector", lambda e, o=o: e.reciprocal(
                    sfx[:, o + 11:o + 12], sfx[:, o + 10:o + 11]), s_V,
                    drain=True)
                tV_g = op("vector", lambda e, o=o, tb=tb: e.tensor_scalar(
                    gat_T[:, tb * 8:(tb + 1) * 8], sfx[:, o:o + 8],
                    sfx[:, o + 11:o + 12], None, ALU.mult), s_V)
            tS_gat = 0
            for tb in range(TBLK):
                pb = 2 + tb % 2
                tT = op("tensor", lambda e, tb=tb, pb=pb: e.transpose(
                    ps[pb][0:8, 0:128], gat_T[:, tb * 8:(tb + 1) * 8],
                    idn[:]), s_T, waits=[(s_V, tV_g), free[pb]])
                tS_gat = op("scalar", lambda e, tb=tb, pb=pb: e.copy(
                    gat[:, tb * 128:(tb + 1) * 128], ps[pb][0:8, 0:128]), s_S,
                    waits=[(s_T, tT), (s_rep, gat_free)])
                free[pb] = (s_S, tS_gat)
            t_gr = dma("gpsimd", grep.ap(),
                       gat.ap().unsqueeze(1).broadcast_to([8, 8, TC]),
                       s_rep, waits=[(s_S, tS_gat), (s_V, midg_rd_prev)])
            gat_free = t_gr
            t_mr = t_gr
            for e8 in range(8):
                t_mr = dma("gpsimd", mrep[e8 * 8:(e8 + 1) * 8, 0:TC],
                           lgr[8:16, :], s_rep, waits=[(s_S, tS_lg)])


            xc_sp = {}
            sz_sp = {}
            tV_cc_done = 0
            silu2 = [0, 0]
            wx_xcb = [0, 0]
            wx_pend = None
            for dblk in range(4, NIC):
                d_pre[dblk % 4].append(dblk)
            for dblk in range(NIC):
                par = dblk % 4
                reg0 = par * NKC * 128
                t_wp = d_hold[dblk]
                if True:
                    pbk = dblk % 2
                    tT = None
                    for kc in range(NKC):
                        tT = op("tensor", lambda e, kc=kc, reg0=reg0,
                                pbk=pbk: e.matmul(
                                    ps[pbk][:, 0:TC],
                                    win_b[:, reg0 + kc * 128:
                                          reg0 + kc * 128 + 128],
                                    xnn_f[:, kc * TC:(kc + 1) * TC],
                                    start=(kc == 0), stop=(kc == NKC - 1)),
                                s_T, waits=([(s_win[par], t_wp), free[pbk]]
                                            if kc == 0 else []))
                    win_mm[par] = tT
                    if d_pre[par]:
                        nxt = d_pre[par].pop(0)
                        d_hold[nxt] = dma(
                            "sync" if par % 2 == 0 else "scalar",
                            win_b[:, reg0:reg0 + NKC * 128].rearrange(
                                "p (c m) -> p c m", m=128),
                            win_sc[:, nxt * 128:(nxt + 1) * 128].rearrange(
                                "(c p) m -> p c m", p=128),
                            s_win[par], waits=[(s_T, win_mm[par])])
                    if wx_pend is not None:
                        pi, pcb = wx_pend
                        wx_pend = None
                        wx_cons = op("tensor", lambda e, i=pi: e.matmul(
                            ps[6][0:33, 0:TC], wx_r[:, i * 33:(i + 1) * 33],
                            xcb[:, (i % 2) * TC:(i % 2 + 1) * TC],
                            start=(i == 0), stop=(i == NIC - 1)), s_T,
                            waits=[(s_V, pcb)] +
                            ([free[6]] if pi == 0 else []))
                        wx_xcb[pi % 2] = wx_cons
                    if dblk < NIC:
                        i = dblk
                        pb = ps[pbk]
                        cdst = tp[:, 0:TC] if i % 2 == 0 else scr1[:, 0:TC]
                        tapw = lambda k, i=i: cw_c[:, 4 * i + k:4 * i + k + 1]
                        op("vector", lambda e, pb=pb, i=i, tapw=tapw,
                           cdst=cdst: e.tensor_scalar(
                               cdst, pb[:, 0:TC], tapw(3),
                               cb2_c[:, i:i + 1], ALU.mult, ALU.add), s_V,
                            waits=[(s_T, tT), (s_S, silu2[i % 2])])
                        for k in (2, 1, 0):
                            nb = 3 - k
                            op("vector", lambda e, pb=pb, k=k, nb=nb,
                               tapw=tapw, cdst=cdst:
                                e.scalar_tensor_tensor(
                                    cdst[:, nb:TC], pb[:, 0:TC - nb], tapw(k),
                                    cdst[:, nb:TC], ALU.mult, ALU.add), s_V)
                            op("vector", lambda e, i=i, k=k, nb=nb, tapw=tapw,
                               cdst=cdst:
                                e.scalar_tensor_tensor(
                                    cdst[:, 0:nb], ccar[:, 4 * i + k:4 * i + 3],
                                    tapw(k), cdst[:, 0:nb], ALU.mult, ALU.add),
                                s_V)
                        tV_cc_done = op("vector", lambda e, pb=pb, i=i: e.tensor_copy(
                            ccar[:, 4 * i:4 * i + 3], pb[:, TC - 3:TC]), s_V)
                        free[pbk] = (s_V, tV_cc_done)
                        if i < 8:
                            xdst = xc_h[:, i * TC:(i + 1) * TC]
                            w_extra = [(s_V, xc_free)] if i == 0 else []
                        else:
                            xdst = tp[:, TC:2 * TC]
                            w_extra = [(s_sp, s_sp.n)]
                        tS_xc = op("scalar", lambda e, xdst=xdst, cdst=cdst:
                                   e.activation(xdst, cdst, AF.Silu), s_S,
                                   waits=[(s_V, tV_cc_done)] + w_extra)
                        silu2[i % 2] = tS_xc
                        if i >= 8:
                            tpA_free = dma("gpsimd", sp_xc3[c * 8 + i - 8],
                                           tp[:, TC:2 * TC], s_sp,
                                           waits=[(s_S, tS_xc)])
                            xc_sp[i] = tpA_free
                        tV_cb = op("vector", lambda e, xdst=xdst, i=i: e.tensor_scalar(
                            xcb[:, (i % 2) * TC:(i % 2 + 1) * TC], xdst, 1.0,
                            None, ALU.mult), s_V,
                            waits=[(s_S, tS_xc), (s_T, wx_xcb[i % 2])])
                        wx_pend = (i, tV_cb)
            if wx_pend is not None:
                pi, pcb = wx_pend
                wx_pend = None
                wx_cons = op("tensor", lambda e, i=pi: e.matmul(
                    ps[6][0:33, 0:TC], wx_r[:, i * 33:(i + 1) * 33],
                    xcb[:, (i % 2) * TC:(i % 2 + 1) * TC],
                    start=(i == 0), stop=(i == NIC - 1)), s_T,
                    waits=[(s_V, pcb)])
                wx_xcb[pi % 2] = wx_cons
            ssm_cons = op("scalar", lambda e: e.copy(
                ssm[0:33, :], ps[6][0:33, 0:TC]), s_S,
                waits=[(s_T, wx_cons), (s_T, ssm_free)])
            tS_smT = 0
            for tb in range(TBLK):
                pb = 2 + tb % 2
                tT = op("tensor", lambda e, tb=tb, pb=pb: e.transpose(
                    ps[pb][:, 0:128], ssm[:, tb * 128:(tb + 1) * 128],
                    idn[:]), s_T,
                    waits=[(s_S, ssm_cons), free[pb], (s_V, ssmT_free)])
                tS_smT = op("scalar", lambda e, tb=tb, pb=pb: e.copy(
                    ssm_T[:, tb * 33:(tb + 1) * 33], ps[pb][:, 0:33]), s_S,
                    waits=[(s_T, tT)])
                free[pb] = (s_S, tS_smT)

            tV_mg = op("vector", lambda e: e.tensor_tensor(
                midg[:], mrep[:, 0:TC], grep[:], ALU.mult), s_V,
                waits=[(s_rep, t_mr),
                       (s_T, max(win_mm))])

            free[6] = (s_S, ssm_cons)

            def mk_z(dblk):
                par = dblk % 4
                reg0 = par * NKC * 128
                hold = {}

                def emit_zd():
                    hold["t"] = dma(
                        "sync" if par % 2 == 0 else "scalar",
                        win_b[:, reg0:reg0 + NKC * 128].rearrange(
                            "p (c m) -> p c m", m=128),
                        win_sc[:, dblk * 128:(dblk + 1) * 128].rearrange(
                            "(c p) m -> p c m", p=128),
                        s_win[par], waits=[(s_T, win_mm[par]),
                                               (s_sp, scratch_ready
                                                if c == 0 else 0)])

                def emit_z():
                    t_wp = hold["t"]
                    pbk = 6 + dblk % 2
                    tT = None
                    for kc in range(NKC):
                        tT = op("tensor", lambda e, kc=kc, reg0=reg0,
                                pbk=pbk: e.matmul(
                                    ps[pbk][:, 0:TC],
                                    win_b[:, reg0 + kc * 128:
                                          reg0 + kc * 128 + 128],
                                    xnn_f[:, kc * TC:(kc + 1) * TC],
                                    start=(kc == 0), stop=(kc == NKC - 1)),
                                s_T, waits=([(s_win[par], t_wp),
                                             free[pbk]]
                                            if kc == 0 else []))
                    win_mm[par] = tT
                    hold["mm"] = tT

                def emit_zf():
                    tT = hold["mm"]
                    pbk = 6 + dblk % 2
                    iz = dblk - NIC
                    tS_z = op("scalar", lambda e, pbk=pbk, iz=iz: e.add(
                        sz_t[:, (iz % 2) * TC:(iz % 2 + 1) * TC],
                        ps[pbk][:, 0:TC],
                        bxz_c[:, NIC + iz:NIC + iz + 1]), s_S,
                        waits=[(s_T, tT), szt_wr[iz % 2]])
                    free[pbk] = (s_S, tS_z)
                    dma("sync", sp_sz3[c * NIC + iz],
                        sz_t[:, (iz % 2) * TC:(iz % 2 + 1) * TC],
                        s_bsp, waits=[(s_S, tS_z)])
                    szt_wr[iz % 2] = (s_bsp, s_bsp.n)
                return emit_zd, emit_z, emit_zf

            def mk_cg(dblk):
                par = dblk % 4
                reg0 = par * NKC * 128
                hold = {}

                def emit_cd():
                    hold["t"] = dma(
                        "sync" if par % 2 == 0 else "scalar",
                        win_b[:, reg0:reg0 + NKC * 128].rearrange(
                            "p (c m) -> p c m", m=128),
                        base_sc[:, dblk * 128:(dblk + 1) * 128].rearrange(
                            "(c p) m -> p c m", p=128),
                        s_win[par], waits=[(s_T, win_mm[par]),
                                               (s_sp, scratch_ready
                                                if c == 0 else 0)])

                def emit_cg():
                    t_bp = hold["t"]
                    pbk = 6 + dblk % 2
                    tT = None
                    for kc in range(NKC):
                        tT = op("tensor", lambda e, kc=kc, reg0=reg0,
                                pbk=pbk: e.matmul(
                                    ps[pbk][:, 0:TC],
                                    win_b[:, reg0 + kc * 128:
                                          reg0 + kc * 128 + 128],
                                    xn_f[:, kc * TC:(kc + 1) * TC],
                                    start=(kc == 0), stop=False), s_T,
                                waits=([(s_win[par], t_bp), free[pbk]]
                                       if kc == 0 else []))
                    tT = op("tensor", lambda e, dblk=dblk, pbk=pbk: e.matmul(
                        ps[pbk][:, 0:TC], lb_r[:, dblk * 128:(dblk + 1) * 128],
                        midg[:], start=False, stop=True), s_T,
                        waits=[(s_V, tV_mg)])
                    win_mm[par] = tT
                    hold["mm"] = tT

                def emit_cf():
                    tT = hold["mm"]
                    pbk = 6 + dblk % 2
                    tS_g = op("scalar", lambda e, dblk=dblk, pbk=pbk:
                              e.add(
                                  g_t[:, (dblk % 2) * TC:(dblk % 2 + 1) * TC],
                                  ps[pbk][:, 0:TC],
                                  bb_c[:, dblk:dblk + 1]), s_S,
                              waits=[(s_T, tT), gt_wr[dblk % 2]])
                    free[pbk] = (s_S, tS_g)
                    dma("sync", sp_g3[c * NKC + dblk],
                        g_t[:, (dblk % 2) * TC:(dblk % 2 + 1) * TC],
                        s_bsp, waits=[(s_S, tS_g)])
                    gt_wr[dblk % 2] = (s_bsp, s_bsp.n)
                return emit_cd, emit_cg, emit_cf

            bg_pairs = ([mk_z(d) for d in range(NIC, 32)] +
                        [mk_cg(d) for d in range(NKC)])
            bg_dmas = [p[0] for p in bg_pairs]
            bg_queue = [p[1] for p in bg_pairs]
            bg_fins = [p[2] for p in bg_pairs]
            bg_nmm = 0
            bg_nfin = 0
            for _ in range(3):
                bg_dmas.pop(0)()
            midg_rd_prev = tV_mg

            ssm_lastT = 0
            for half in range(2):
                if half == 1:
                    t_xcl = 0
                    for j in range(8):
                        t_xcl = dma("sync", xc_h[:, j * TC:(j + 1) * TC],
                                    sp_xc3[c * 8 + j], s_xcl,
                                    waits=[(s_sp, s_sp.n),
                                           (s_V, xc_free)])
                tT_bc = op("tensor", lambda e: e.matmul(
                    ps[4][:, 0:TC], onesr[0:1, 0:128], ssm[0:1, 0:TC],
                    start=True, stop=True), s_T,
                    waits=[(s_S, ssm_cons), (s_ldw, t_small), free[4]])
                ssm_lastT = tT_bc
                tS_dt = 0
                tV_u = 0
                u_j = [0] * 8
                for j in range(8):
                    i = half * 8 + j
                    op("scalar", lambda e, i=i: e.activation(
                        e1t[:], ps[4][:, 0:TC], AF.Exp,
                        bias=bdt_c[:, i:i + 1], scale=wdt_c[:, i:i + 1]), s_S,
                        waits=[(s_T, tT_bc)] +
                        ([(s_V, dt_free), (s_T, scr1_freeT)]
                         if j == 0 else []) +
                        ([(s_sp, scratch_ready)]
                         if (c == 0 and half == 0 and j == 0) else []))
                    tS_dt = op("scalar", lambda e, j=j: e.activation(
                        dt_h[:, j * TC:(j + 1) * TC], e1t[:], AF.Ln,
                        bias=1.0), s_S)
                    w_u = [(s_S, tS_dt)]
                    if half == 1 and j == 0:
                        w_u.append((s_xcl, t_xcl))
                    if c == 0 and half == 0 and j == 0:
                        w_u.append((s_sp, scratch_ready))
                    tV_u = op("vector", lambda e, j=j: e.tensor_tensor(
                        u_h[:, j * TC:(j + 1) * TC],
                        dt_h[:, j * TC:(j + 1) * TC],
                        xc_h[:, j * TC:(j + 1) * TC], ALU.mult), s_V,
                        waits=w_u)
                    u_j[j] = tV_u
                free[4] = (s_S, tS_dt)
                dt_free = tV_u

                ymT_prev = ymT_done
                def emit_bbc(n):
                    bbk = 4 + n % 2
                    slot = 32 * (n % 3)
                    t_brow = dma("gpsimd", bc_st[slot:slot + 1, :],
                                 ssm[1 + n:2 + n, 0:TC], s_bc[n % 3],
                                 waits=[(s_S, ssm_cons),
                                        (s_T, bc_slot_free[n % 3])])
                    tT_bb = op("tensor", lambda e, n=n, bbk=bbk, slot=slot:
                               e.matmul(
                                   ps[bbk][:, 0:TC],
                                   onesr[slot:slot + 1, 0:128],
                                   bc_st[slot:slot + 1, 0:TC],
                                   start=True, stop=True), s_T,
                               waits=[(s_bc[n % 3], t_brow), free[bbk]])
                    bc_slot_free[n % 3] = tT_bb
                    tS_bev = op("scalar", lambda e, n=n, bbk=bbk: e.copy(
                        b_rep[:, (n % 3) * TC:(n % 3 + 1) * TC],
                        ps[bbk][:, 0:TC]), s_S,
                        waits=[(s_T, tT_bb), (s_V, brep_cons[n % 3])])
                    free[bbk] = (s_S, tS_bev)
                    return (tT_bb, tS_bev)

                def emit_ym(n, qps, tts_q):
                    lastT = {0: None, 1: None}
                    tV_ym = 0
                    for tb in range(TBLK):
                        pb = (2, 3, 0, 1)[tb]
                        tT_h = None
                        for q in range(2):
                            qp = qps[q]
                            for jj in range(4):
                                tT_h = op("tensor", lambda e, qp=qp, jj=jj,
                                          tb=tb, pb=pb, q=q: e.transpose(
                                              ps[pb].ap().bitcast(BF16)[
                                                  :, q * 512 + jj * 128:
                                                  q * 512 + jj * 128 + 128],
                                              h_q[:, qp * QW + jj * 513 + 1 +
                                                  tb * 128:qp * QW + jj * 513
                                                  + 1 + tb * 128 + 128],
                                              idnb[:]), s_T,
                                          waits=([(s_V, tts_q[q]), free[pb]]
                                                 if jj == 0 else []))
                            lastT[q] = tT_h
                        ysl = tb * 1024
                        if n == 0:
                            tV_ym = op("vector", lambda e, ysl=ysl,
                                       pb=pb, tb=tb, n=n: e.tensor_scalar(
                                           y_Th[:, ysl:ysl + 1024],
                                           ps[pb].ap().bitcast(BF16)[:, 0:1024],
                                           ssm_T[:, tb * 33 + 17 + n:
                                                 tb * 33 + 18 + n],
                                           None, ALU.mult), s_V,
                                       waits=[(s_T, tT_h),
                                              (s_S, tS_smT),
                                              (s_T, ymT_prev)])
                        else:
                            tV_ym = op("vector", lambda e, ysl=ysl,
                                       pb=pb, tb=tb, n=n:
                                       e.scalar_tensor_tensor(
                                           y_Th[:, ysl:ysl + 1024],
                                           ps[pb].ap().bitcast(BF16)[:, 0:1024],
                                           ssm_T[:, tb * 33 + 17 + n:
                                                 tb * 33 + 18 + n],
                                           y_Th[:, ysl:ysl + 1024],
                                           ALU.mult, ALU.add), s_V,
                                       waits=[(s_T, tT_h)])
                        free[pb] = (s_V, tV_ym)
                    tts_hfree[qps[0]] = lastT[0]
                    tts_hfree[qps[1]] = lastT[1]
                    return tV_ym

                pend = None
                tT_bb_pre = emit_bbc(0)
                for n in range(NST):
                    bbk = 4 + n % 2
                    tT_bb, tS_bev = tT_bb_pre
                    ssm_lastT = tT_bb
                    if n + 1 < NST:
                        tT_bb_pre = emit_bbc(n + 1)
                    tts_q = [0, 0]
                    qps = [0, 0]
                    for q in range(2):
                        parq = abq_count % 2
                        para = abq_count % 3
                        qp = qi_count % 4
                        qps[q] = qp
                        abq_count += 1
                        qi_count += 1
                        tS_ab = 0
                        for jj in range(4):
                            j = q * 4 + jj
                            i = half * 8 + j
                            tS_ab = op("scalar", lambda e, j=j, i=i, n=n,
                                       para=para, jj=jj: e.activation(
                                           abar[:, para * QW + jj * 513 + 1:
                                                para * QW + jj * 513 + 513],
                                           dt_h[:, j * TC:(j + 1) * TC],
                                           AF.Exp,
                                           scale=a_c[:, i * NST + n:
                                                     i * NST + n + 1]),
                                       s_S, waits=([(s_V, ab_free[para])]
                                                   if jj == 0 else []))
                        bdst = beta.ap().rearrange(
                            "p (x s) -> p x s", s=513)[
                            :, 4 * parq:4 * parq + 4, 1:513]
                        bsrc_u = u_h.ap().rearrange(
                            "p (x t) -> p x t", t=TC)[
                            :, q * 4:q * 4 + 4, :]
                        bsrc_b = b_rep.ap()[
                            :, (n % 3) * TC:(n % 3 + 1) * TC].unsqueeze(
                            1).broadcast_to([128, 4, TC])
                        tV_be = op("vector", lambda e, bdst=bdst,
                                   bsrc_u=bsrc_u, bsrc_b=bsrc_b:
                                   e.tensor_tensor(bdst, bsrc_u, bsrc_b,
                                                   ALU.mult), s_V,
                                   waits=[(s_S, tS_bev)])
                        csrc = carry.ap().rearrange(
                            "p (i n2) -> p i n2", n2=NST)[
                            :, half * 8 + q * 4:half * 8 + q * 4 + 4, n]
                        bspc = beta.ap().rearrange(
                            "p (x s) -> p x s", s=513)[
                            :, 4 * parq:4 * parq + 4, 0]
                        tS_spc = op("scalar", lambda e, bspc=bspc, csrc=csrc:
                                    e.copy(bspc, csrc), s_S)
                        tts_q[q] = op("vector", lambda e, parq=parq, qp=qp,
                                      para=para:
                                      e.tensor_tensor_scan(
                                          h_q[:, qp * QW:(qp + 1) * QW],
                                          abar[:, para * QW:(para + 1) * QW],
                                          beta[:, parq * QW:(parq + 1) * QW],
                                          0.0, ALU.mult, ALU.add), s_V,
                                      waits=[(s_S, tS_ab), (s_S, tS_spc),
                                             (s_T, tts_hfree[qp]),
                                             (s_S, carry_hfree[qp])])
                        ab_free[para] = tts_q[q]
                    qp0 = qps[0]
                    csr2 = h_q.ap()[
                        :, qp0 * QW:(qp0 + 2) * QW].rearrange(
                        "p (x s) -> p x s", s=513)[:, :, 512]
                    cdst = carry.ap().rearrange(
                        "p (i n2) -> p i n2", n2=NST)[
                        :, half * 8:half * 8 + 8, n]
                    carry_sval = op("scalar", lambda e, csr2=csr2,
                                    cdst=cdst: e.copy(cdst, csr2), s_S,
                                    waits=[(s_V, tts_q[1])])
                    carry_hfree[qps[0]] = carry_sval
                    carry_hfree[qps[1]] = carry_sval
                    brep_cons[n % 3] = tts_q[1]
                    if bg_dmas:
                        bg_dmas.pop(0)()
                    if bg_queue:
                        bg_queue.pop(0)()
                        bg_nmm += 1
                    if bg_nmm > bg_nfin:
                        bg_fins.pop(0)()
                        bg_nfin += 1
                    if pend is not None:
                        tV_ym = emit_ym(*pend)
                    pend = (n, qps, tts_q)
                tV_ym = emit_ym(*pend)
                pend = None
                ssmT_free = tV_ym
                if half == 1:
                    w_hold = {}
                    for dblk in range(2):
                        par = dblk % 2
                        reg0 = par * NIC * 128
                        w_hold[dblk] = dma(
                            "scalar",
                            wob_b[:, reg0:reg0 + NIC * 128].rearrange(
                                "p (c m) -> p c m", m=128),
                            wout_sc[:, dblk * 128:(dblk + 1) * 128].rearrange(
                                "(c p) m -> p c m", p=128),
                            s_wob[par], waits=[(s_T, wob_mm[par]),
                                               (s_sp, scratch_ready
                                                if c == 0 else 0)])

                tV_y3 = 0
                for j in range(8):
                    i = half * 8 + j
                    pb = 2 + j % 2
                    tT_y = None
                    for tb in range(TBLK):
                        tT_y = op("tensor", lambda e, j=j, tb=tb, pb=pb:
                                  e.transpose(
                                      ps[pb][:, tb * 128:(tb + 1) * 128],
                                      y_Th[:, tb * 1024 + j * 128:
                                           tb * 1024 + j * 128 + 128],
                                      idn[:]), s_T,
                                  waits=([(s_V, tV_ym), free[pb]]
                                         if tb == 0 else []))
                    ymT_done = tT_y
                    t_szl = dma("sync", sz_t[:, (j % 2) * TC:(j % 2 + 1) * TC],
                                sp_sz3[c * NIC + i], s_szl[j % 2],
                                waits=[(s_bsp, s_bsp.n), szt_wr[j % 2]])
                    tS_sl = op("scalar", lambda e, j=j: e.activation(
                        sz_t[:, (j % 2) * TC:(j % 2 + 1) * TC],
                        sz_t[:, (j % 2) * TC:(j % 2 + 1) * TC], AF.Silu),
                        s_S, waits=[(s_szl[j % 2], t_szl)])
                    tV_y2 = op("vector", lambda e, j=j, i=i, pb=pb:
                               e.scalar_tensor_tensor(
                                   tp2[:], xc_h[:, j * TC:(j + 1) * TC],
                                   d_c[:, i:i + 1], ps[pb][:, 0:TC],
                                   ALU.mult, ALU.add), s_V,
                               waits=[(s_T, tT_y)])
                    free[pb] = (s_V, tV_y2)
                    tV_y3 = op("vector", lambda e, j=j, i=i: e.tensor_tensor(
                        y3[:, i * TC:(i + 1) * TC], tp2[:],
                        sz_t[:, (j % 2) * TC:(j % 2 + 1) * TC], ALU.mult),
                        s_V, waits=[(s_S, tS_sl)] +
                        ([(s_T, y3_free)] if (half == 0 and j == 0) else []))
                    szt_wr[j % 2] = (s_V, tV_y3)
                xc_free = tV_y3

            if dbg_on:
                dma("gpsimd", dbg_y3.rearrange("(c p) m -> c p m", p=128)[c],
                    y3.ap(), s_sp, waits=[(s_V, xc_free)])

            while bg_dmas:
                bg_dmas.pop(0)()
            while bg_queue:
                bg_queue.pop(0)()
            while bg_fins:
                bg_fins.pop(0)()
            if c + 1 < NCH:
                t_x = dma("sync",
                          x_T.ap().rearrange("p (b d) -> p b d", d=DM),
                          x_t3[c + 1].rearrange("b p d -> p b d"),
                          s_ldx, waits=[(s_V, xT_rd_done[0]),
                                        (s_S, xT_rd_done[1])])
                pend_A = emit_A_parts(c + 1)
            else:
                pend_A = []
            xnmm = [win_mm[0], win_mm[1]]
            for dblk in range(NKC):
                par = dblk % 2
                reg0 = par * NIC * 128
                t_wo = w_hold[dblk]
                t_xn = dma("scalar",
                           win_b[:, par * 1024:par * 1024 + 512],
                           xn_sc2[c % 2][dblk * 128:(dblk + 1) * 128, :],
                           s_xnl[par], waits=[(s_xnw, s_xnw.n),
                                              (s_T, xnmm[par])])
                pbk = 2 + dblk % 2
                tT = None
                for ki in range(NIC):
                    tT = op("tensor", lambda e, ki=ki, reg0=reg0, pbk=pbk:
                            e.matmul(
                                ps[pbk][:, 0:TC],
                                wob_b[:, reg0 + ki * 128:reg0 + ki * 128 + 128],
                                y3[:, ki * TC:(ki + 1) * TC],
                                start=(ki == 0), stop=False), s_T,
                            waits=([(s_wob[par], t_wo), (s_V, xc_free),
                                    free[pbk]] if ki == 0 else []))
                tT = op("tensor", lambda e, pbk=pbk, par=par: e.matmul(
                    ps[pbk][:, 0:TC], idnb[:],
                    win_b[:, par * 1024:par * 1024 + 512],
                    start=False, stop=True), s_T,
                    waits=[(s_xnl[par], t_xn)])
                xnmm[par] = tT
                if dblk + 2 < NKC:
                    w_hold[dblk + 2] = dma(
                        "scalar",
                        wob_b[:, reg0:reg0 + NIC * 128].rearrange(
                            "p (c m) -> p c m", m=128),
                        wout_sc[:, (dblk + 2) * 128:(dblk + 3) * 128].rearrange(
                            "(c p) m -> p c m", p=128),
                        s_wob[par], waits=[(s_T, tT)])
                wob_mm[par] = tT
                y3_free = tT
                tS_s = op("scalar", lambda e, pbk=pbk, par=par: e.copy(
                    sz_t[:, par * TC:(par + 1) * TC],
                    ps[pbk][:, 0:TC]), s_S,
                    waits=[(s_T, tT), szt_wr[par]])
                free[pbk] = (s_S, tS_s)
                t_gl = dma("sync", g_t[:, par * TC:(par + 1) * TC],
                           sp_g3[c * NKC + dblk], s_gl[par],
                           waits=[(s_bsp, s_bsp.n), gt_wr[par]])
                tS_gsl = op("scalar", lambda e, par=par: e.activation(
                    g_t[:, par * TC:(par + 1) * TC],
                    g_t[:, par * TC:(par + 1) * TC], AF.Silu), s_S,
                    waits=[(s_gl[par], t_gl)])
                t_xr = dma("sync",
                           x_rt[:, par * TC:(par + 1) * TC].rearrange(
                               "p (b d) -> p b d", d=128),
                           x_t3[c].rearrange("b p d -> p b d")[
                               :, :, dblk * 128:(dblk + 1) * 128],
                           s_xrt[par], waits=[(s_V, o3_prev[par])])
                pbs = 4 + dblk % 2
                pbg = 4 + (dblk + 1) % 2
                tT_s = None
                for tb in range(TBLK):
                    tT_s = op("tensor", lambda e, tb=tb, par=par, pbs=pbs:
                              e.transpose(
                                  ps[pbs][:, tb * 128:(tb + 1) * 128],
                                  sz_t[:, par * TC + tb * 128:
                                       par * TC + tb * 128 + 128],
                                  idn[:]), s_T,
                              waits=([(s_S, tS_s), free[pbs]]
                                     if tb == 0 else []))
                szt_wr[par] = (s_T, tT_s)
                tT_g = None
                for tb in range(TBLK):
                    tT_g = op("tensor", lambda e, tb=tb, par=par, pbg=pbg:
                              e.transpose(
                                  ps[pbg][:, tb * 128:(tb + 1) * 128],
                                  g_t[:, par * TC + tb * 128:
                                      par * TC + tb * 128 + 128],
                                  idn[:]), s_T,
                              waits=([(s_S, tS_gsl), free[pbg]]
                                     if tb == 0 else []))
                gt_wr[par] = (s_T, tT_g)
                osl = ostage[:, par * TC:(par + 1) * TC].rearrange(
                    "p (b d) -> p b d", d=128)
                xrl = x_rt[:, par * TC:(par + 1) * TC].rearrange(
                    "p (b d) -> p b d", d=128)
                ps_s = ps[pbs].ap().rearrange("p (b d) -> p b d", d=128)
                gT = tp[:, par * TC:(par + 1) * TC]
                tS_g2 = op("scalar", lambda e, gT=gT, pbg=pbg: e.copy(
                    gT, ps[pbg][:, 0:TC]), s_S, waits=[(s_T, tT_g)])
                free[pbg] = (s_S, tS_g2)
                op("vector", lambda e, osl=osl, ps_s=ps_s, gT=gT:
                   e.tensor_tensor(osl, ps_s,
                                   gT.rearrange("p (b d) -> p b d", d=128),
                                   ALU.mult), s_V,
                   waits=[(s_T, tT_s), (s_S, tS_g2),
                          (s_ost[par], s_ost[par].n)])
                free[pbs] = (s_V, s_V.n)
                tV_o = op("vector", lambda e, osl=osl, xrl=xrl:
                          e.tensor_tensor(osl, osl, xrl, ALU.add), s_V,
                          waits=[(s_xrt[par], t_xr)])
                o3_prev[par] = tV_o
                dma("gpsimd",
                    out_t3[c].rearrange("b p d -> p b d")[
                        :, :, dblk * 128:(dblk + 1) * 128],
                    ostage[:, par * TC:(par + 1) * TC].rearrange(
                        "p (b d) -> p b d", d=128),
                    s_ost[par], waits=[(s_V, tV_o)])
                gt_wr[par] = (s_ost[par], s_ost[par].n)
                if dblk % 3 == 2 and pend_A:
                    pend_A.pop(0)()
            while pend_A:
                pend_A.pop(0)()
            ssm_free = ssm_lastT
            win_mm[0] = max(win_mm[0], xnmm[0])
            win_mm[1] = max(win_mm[1], xnmm[1])

        B.wait("sync", s_ost[0], s_ost[0].n)
        B.wait("sync", s_ost[1], s_ost[1].n)

        stk.enter_context(nc.allow_non_contiguous_dma(
            reason="small one-time vector layout loads"))
        for eng_name in Builder.ENGINES:
            deco = getattr(block, eng_name)

            def make(fns):
                def body(engine):
                    for f in fns:
                        f(engine)
                return body
            deco(make(B.ops[eng_name]))

    return nc


_PROG_CACHE = {}


def _get_prog(T):
    if T not in _PROG_CACHE:
        _PROG_CACHE[T] = build_program(T)
    return _PROG_CACHE[T]


def kernel(**inputs):
    x = np.asarray(inputs["L0_attn_x"])
    B_, T, _ = x.shape
    asc = np.ascontiguousarray
    w = {
        "winT": asc(np.asarray(inputs["L0_attn_W_in"]).T),
        "woutT": asc(np.asarray(inputs["L0_attn_W_out"]).T),
        "baseT": asc(np.asarray(inputs["L0_ffn_base_w"]).T),
        "wxT": asc(np.asarray(inputs["L0_attn_W_x"]).T),
        "loraB": asc(np.asarray(inputs["L0_ffn_lora_B"]).reshape(64, DM)),
        "rlT": asc(np.concatenate(
            [np.asarray(inputs["L0_ffn_router_w"]).T,
             np.asarray(inputs["L0_ffn_lora_A"])], axis=1)),
        "ln_g": asc(np.asarray(inputs["L0_attn_ln_gamma"])),
        "ln_b": asc(np.asarray(inputs["L0_attn_ln_beta"])),
        "conv_w": asc(np.asarray(inputs["L0_attn_conv_w"])),
        "conv_b": asc(np.asarray(inputs["L0_attn_conv_b"])),
        "w_dt": asc(np.asarray(inputs["L0_attn_w_dt"])),
        "b_dt": asc(np.asarray(inputs["L0_attn_b_dt"])),
        "a_log": asc(np.asarray(inputs["L0_attn_A_log"])),
        "d_prm": asc(np.asarray(inputs["L0_attn_D_param"])),
        "base_b": asc(np.asarray(inputs["L0_ffn_base_b"])),
        "rout_b": np.concatenate(
            [np.asarray(inputs["L0_ffn_router_b"], np.float32).reshape(8, 1),
             np.zeros((8, 1), np.float32)]),
        "ident": np.eye(128, dtype=np.float32),
        "identb": np.eye(128, dtype=ml_dtypes.bfloat16),
        "ones1": np.ones((96, 128), dtype=np.float32),
        "cvals": np.tile(np.array([[1e-5, -0.5, -1.0, 1.0]], np.float32),
                         (128, 1)),
    }
    nc = _get_prog(T)
    in_maps = [dict(w, x=asc(x[b])) for b in range(B_)]
    res = run_bass_kernel_spmd(nc, in_maps, core_ids=list(range(B_)))
    return np.stack([np.asarray(r["out"]) for r in res.results], axis=0)


# version 28
# speedup vs baseline: 1.0261x; 1.0261x over previous
import contextlib
import os
import numpy as np
import ml_dtypes
import concourse.bass as bass
import concourse.mybir as mybir
from concourse.bass_utils import run_bass_kernel_spmd

F32 = mybir.dt.float32
BF16 = mybir.dt.bfloat16
AF = mybir.ActivationFunctionType
ALU = mybir.AluOpType
AX = mybir.AxisListType

DM, DI, NST = 1024, 2048, 16
NKC, NIC = DM // 128, DI // 128
TC, TBLK = 512, 4
LN_EPS = 1e-5
N_CORES = 8


class Sem:
    def __init__(self, handle, step=1):
        self.h, self.n, self.step = handle, 0, step


class Builder:
    ENGINES = ("sync", "gpsimd", "vector", "scalar", "tensor")

    def __init__(self):
        self.ops = {e: [] for e in self.ENGINES}

    def dma(self, eng, dst_ap, src_ap, sem, waits=()):
        sem.n += 16
        tgt = sem.n
        w = tuple((s.h, v) for (s, v) in waits if v)

        def f(e, dst_ap=dst_ap, src_ap=src_ap, w=w, sem=sem):
            for (h, v) in w:
                e.wait_ge(h, v)
            e.dma_start(out=dst_ap, in_=src_ap).then_inc(sem.h, 16)
        self.ops[eng].append(f)
        return tgt

    def op(self, eng, fn, sem=None, waits=(), drain=False):
        if sem is not None:
            sem.n += 1
        tgt = sem.n if sem is not None else None
        w = tuple((s.h, v) for (s, v) in waits if v)
        assert not (drain and sem is None)

        def f(e, fn=fn, w=w, sem=sem, tgt=tgt, drain=drain):
            for (h, v) in w:
                e.wait_ge(h, v)
            ins = fn(e)
            if sem is not None:
                ins.then_inc(sem.h, 1)
            if drain:
                e.wait_ge(sem.h, tgt)
        self.ops[eng].append(f)
        return tgt

    def wait(self, eng, sem, val):
        if val:
            def f(e, sem=sem, val=val):
                e.wait_ge(sem.h, val)
            self.ops[eng].append(f)


def build_program(T):
    NCH = T // TC
    nc = bass.Bass("TRN2", target_bir_lowering=False, debug=False,
                   detect_race_conditions=False)

    def din(name, shape, dtype=F32):
        return nc.dram_tensor(name, shape, dtype, kind="ExternalInput").ap()

    x_hbm = din("x", [T, DM])
    winT = din("winT", [DM, 2 * DI])
    woutT = din("woutT", [DI, DM])
    baseT = din("baseT", [DM, DM])
    wxT = din("wxT", [DI, 33])
    loraB = din("loraB", [64, DM])
    rlT = din("rlT", [DM, 16])
    ln_g = din("ln_g", [DM])
    ln_b = din("ln_b", [DM])
    conv_w = din("conv_w", [DI, 4])
    conv_b = din("conv_b", [DI])
    w_dt = din("w_dt", [DI, 1])
    b_dt = din("b_dt", [DI])
    a_log = din("a_log", [DI, NST])
    d_prm = din("d_prm", [DI])
    base_b = din("base_b", [DM])
    rout_b = din("rout_b", [16, 1])
    ident = din("ident", [128, 128])
    identb = din("identb", [128, 128], BF16)
    ones1 = din("ones1", [96, 128])
    cvals = din("cvals", [128, 4])

    out_hbm = nc.dram_tensor("out", [T, DM], F32, kind="ExternalOutput").ap()
    dbg_kind = "ExternalOutput" if os.environ.get("KDBG") else "Internal"
    win_sc = nc.dram_tensor("win_sc", [DM, 2 * DI], BF16, kind="Internal").ap()
    wout_sc = nc.dram_tensor("wout_sc", [DI, DM], BF16, kind="Internal").ap()
    base_sc = nc.dram_tensor("base_sc", [DM, DM], BF16, kind="Internal").ap()
    sp_xc = nc.dram_tensor("sp_xc", [NCH * 8 * 128, TC], F32, kind=dbg_kind).ap()
    dbg_on = dbg_kind == "ExternalOutput"
    if dbg_on:
        dbg_xnf = nc.dram_tensor("dbg_xnf", [NCH * 128, NKC * TC], BF16,
                                 kind="ExternalOutput").ap()
        dbg_xnnf = nc.dram_tensor("dbg_xnnf", [NCH * 128, NKC * TC], BF16,
                                  kind="ExternalOutput").ap()
        dbg_y3 = nc.dram_tensor("dbg_y3", [NCH * 128, NIC * TC], BF16,
                                kind="ExternalOutput").ap()
    sp_sz = nc.dram_tensor("sp_sz", [NCH * NIC * 128, TC], F32, kind=dbg_kind).ap()
    sp_g = nc.dram_tensor("sp_g", [NCH * NKC * 128, TC], F32, kind=dbg_kind).ap()

    with contextlib.ExitStack() as stk:
        En = stk.enter_context

        def sbuf(name, shape, dtype=F32):
            return En(nc.sbuf_tensor(name, shape, dtype))

        x_T = sbuf("x_T", [128, TBLK * DM])
        xn_T = sbuf("xn_T", [128, DM])
        xn_f = sbuf("xn_f", [128, NKC * TC], BF16)
        xnn_f = sbuf("xnn_f", [128, NKC * TC], BF16)
        scr1 = sbuf("scr1", [128, DM])
        xc_h = sbuf("xc_h", [128, 8 * TC])
        dt_h = sbuf("dt_h", [128, 8 * TC])
        u_h = sbuf("u_h", [128, 8 * TC], BF16)
        QW = 4 * (TC + 1)
        abar = sbuf("abar", [128, 3 * QW], BF16)
        beta = sbuf("beta", [128, 2 * QW], BF16)
        h_q = sbuf("h_q", [128, 4 * QW], BF16)
        carry = sbuf("carry", [128, NIC * NST])
        y_Th = sbuf("y_Th", [128, TBLK * 8 * 128])
        y3 = sbuf("y3", [128, NIC * TC], BF16)
        ssm = sbuf("ssm", [128, TC])
        ssm_T = sbuf("ssm_T", [128, TBLK * 33])
        win_b = sbuf("win_b", [128, 4 * NKC * 128], BF16)
        wob_b = sbuf("wob_b", [128, 2 * NIC * 128], BF16)
        wx_r = sbuf("wx_r", [128, NIC * 33], BF16)
        lb_r = sbuf("lb_r", [64, DM], BF16)
        rl_r = sbuf("rl_r", [128, NKC * 16], BF16)
        g_t = sbuf("g_t", [128, 2 * TC])
        sz_t = sbuf("sz_t", [128, 2 * TC])
        tp = sbuf("tp", [128, 2 * TC])
        xcb = sbuf("xcb", [128, 2 * TC], BF16)
        lgr = sbuf("lgr", [128, TC])
        lm_T = sbuf("lm_T", [128, TBLK * 16])
        sfx = sbuf("sfx", [128, TBLK * 12])
        gat_T = sbuf("gat_T", [128, TBLK * 8])
        gat = sbuf("gat", [8, TC], BF16)
        grep = sbuf("grep", [64, TC])
        mrep = sbuf("mrep", [64, TC])
        midg = sbuf("midg", [64, TC], BF16)
        stat = sbuf("stat", [128, 32])
        idn = sbuf("idn", [128, 128])
        idnb = sbuf("idnb", [128, 128], BF16)
        x_rt = sbuf("x_rt", [128, 2 * TC])
        onesr = sbuf("onesr", [96, 128])
        gam_c = sbuf("gam_c", [128, NKC])
        bet_c = sbuf("bet_c", [128, NKC])
        cw_c = sbuf("cw_c", [128, NIC * 4])
        cb_c = sbuf("cb_c", [128, NIC])
        cb2_c = sbuf("cb2_c", [128, NIC])
        wsum_c = sbuf("wsum_c", [128, NIC])
        wdt_c = sbuf("wdt_c", [128, NIC])
        bdt_c = sbuf("bdt_c", [128, NIC])
        a_c = sbuf("a_c", [128, NIC * NST])
        d_c = sbuf("d_c", [128, NIC])
        bb_c = sbuf("bb_c", [128, NKC])
        bxz_c = sbuf("bxz_c", [128, 2 * NIC])
        rb_c = sbuf("rb_c", [16, 1])
        cv_s = sbuf("cv_s", [128, 4])
        bc_st = sbuf("bc_st", [96, TC])
        b_rep = sbuf("b_rep", [128, 3 * TC], BF16)
        ccar = sbuf("ccar", [128, NIC * 4])
        stg = dt_h
        stgb = u_h.ap()
        e1t = scr1[:, 0:TC]
        scr1b = scr1.ap().bitcast(BF16)
        ostage = g_t
        tp2 = scr1[:, TC:2 * TC]

        ps = [En(nc.psum_tensor(f"ps{i}", [128, 512], F32)) for i in range(8)]

        s_ldx = Sem(En(nc.semaphore("s_ldx")), 16)
        s_ldw = Sem(En(nc.semaphore("s_ldw")), 16)
        s_sp = Sem(En(nc.semaphore("s_sp")), 16)
        s_bsp = Sem(En(nc.semaphore("s_bsp")), 16)
        s_out = Sem(En(nc.semaphore("s_out")), 16)
        s_win = [Sem(En(nc.semaphore(f"s_win{i}")), 16) for i in range(4)]
        s_wob = [Sem(En(nc.semaphore(f"s_wob{i}")), 16) for i in range(2)]
        s_gl = [Sem(En(nc.semaphore(f"s_gl{i}")), 16) for i in range(2)]
        s_szl = [Sem(En(nc.semaphore(f"s_szl{i}")), 16) for i in range(2)]
        s_xcl = Sem(En(nc.semaphore("s_xcl")), 16)
        s_bc = [Sem(En(nc.semaphore(f"s_bc{i}")), 16) for i in range(3)]
        s_rep = Sem(En(nc.semaphore("s_rep")), 16)
        s_xrt = [Sem(En(nc.semaphore(f"s_xrt{i}")), 16) for i in range(2)]
        s_ost = [Sem(En(nc.semaphore(f"s_ost{i}")), 16) for i in range(2)]
        s_T = Sem(En(nc.semaphore("s_T")))
        s_V = Sem(En(nc.semaphore("s_V")))
        s_S = Sem(En(nc.semaphore("s_S")))
        block = En(nc.Block())

        B = Builder()
        dma, op = B.dma, B.op
        free = {i: (s_V, 0) for i in range(8)}
        nc.const_aps.register(F32, LN_EPS, cv_s[:, 0:1])
        nc.const_aps.register(F32, -0.5, cv_s[:, 1:2])
        nc.const_aps.register(F32, -1.0, cv_s[:, 2:3])
        nc.const_aps.register(F32, 1.0, cv_s[:, 3:4])

        dma("sync", gam_c.ap(), ln_g.rearrange("(c p) -> p c", p=128), s_ldw)
        t_bet = dma("sync", bet_c.ap(),
                    ln_b.rearrange("(c p) -> p c", p=128), s_ldw)
        dma("sync", cw_c.ap().rearrange("p (c k) -> p c k", k=4),
            conv_w.rearrange("(c p) k -> p c k", p=128), s_ldw)
        dma("sync", cb_c.ap(), conv_b.rearrange("(c p) -> p c", p=128), s_ldw)
        dma("sync", wdt_c.ap().unsqueeze(2),
            w_dt.rearrange("(c p) o -> p c o", p=128), s_ldw)
        dma("sync", bdt_c.ap(), b_dt.rearrange("(c p) -> p c", p=128), s_ldw)
        t_al = dma("sync", a_c.ap().rearrange("p (c n) -> p c n", n=NST),
                   a_log.rearrange("(c p) n -> p c n", p=128), s_ldw)
        dma("sync", d_c.ap(), d_prm.rearrange("(c p) -> p c", p=128), s_ldw)
        dma("sync", bb_c.ap(), base_b.rearrange("(c p) -> p c", p=128), s_ldw)
        t_rb = dma("sync", rb_c[0:16, :], rout_b, s_ldw)
        t_id = dma("sync", idn.ap(), ident, s_ldw)
        dma("sync", idnb.ap(), identb, s_ldw)
        dma("sync", cv_s.ap(), cvals, s_ldw)
        t_on = dma("sync", onesr.ap(), ones1, s_ldw)
        t_wx = dma("sync",
                   stg[:, 0:NIC * 33].rearrange("p (c m) -> p c m", m=33),
                   wxT.rearrange("(c p) m -> p c m", p=128), s_ldw)
        t_lb = dma("sync", xc_h[0:64, 0:DM], loraB, s_ldw)
        t_rl = dma("sync",
                   stg[:, NIC * 33:NIC * 33 + NKC * 16].rearrange(
                       "p (c m) -> p c m", m=16),
                   rlT.rearrange("(c p) m -> p c m", p=128), s_ldw)
        t_small = s_ldw.n

        op("vector", lambda e: e.memset(ssm[:], 0.0), s_V)
        op("vector", lambda e: e.memset(lgr[:], 0.0), s_V)
        op("vector", lambda e: e.memset(carry[:], 0.0), s_V)
        op("vector", lambda e: e.memset(abar[:], 0.0), s_V)
        op("scalar", lambda e: e.activation(a_c[:], a_c[:], AF.Exp),
           s_S, waits=[(s_ldw, t_small)], drain=True)
        op("scalar", lambda e: e.mul(a_c[:], a_c[:], -1.0), s_S)
        tV_wx = op("vector", lambda e: e.tensor_copy(wx_r[:], stg[:, 0:NIC * 33]),
                   s_V, waits=[(s_ldw, t_small)])
        tV_lb = op("vector", lambda e: e.tensor_scalar(
            lb_r[:], xc_h[0:64, 0:DM], 2.0, None, ALU.mult), s_V,
            waits=[(s_ldw, t_small)])
        tV_rl = op("vector", lambda e: e.tensor_copy(
            rl_r[:], stg[:, NIC * 33:NIC * 33 + NKC * 16]), s_V,
            waits=[(s_ldw, t_small)])

        last_sp = 0
        tSb = None
        B.wait("sync", s_ldw, t_small)
        for pc in range(8):
            cols = slice(pc * 512, (pc + 1) * 512)
            t_pc = dma("sync",
                       stg.ap().rearrange("p (c m) -> p c m", m=512),
                       winT[:, cols].rearrange("(c p) m -> p c m", p=128),
                       s_ldw, waits=[(s_sp, last_sp),
                                     (s_V, tV_rl if pc == 0 else 0),
                                     (s_V, tV_wx if pc == 0 else 0)])
            tV = None
            for kc in range(NKC):
                tV = op("vector", lambda e, kc=kc: e.tensor_scalar(
                    stgb[:, kc * 512:(kc + 1) * 512],
                    stg[:, kc * 512:(kc + 1) * 512],
                    gam_c[:, kc:kc + 1], None, ALU.mult), s_V,
                    waits=[(s_ldw, t_pc)])
            for m2 in range(4):
                tT = None
                for kc in range(NKC):
                    tT = op("tensor", lambda e, kc=kc, m2=m2: e.matmul(
                        ps[7][:, 0:1],
                        stg[:, kc * 512 + m2 * 128:kc * 512 + m2 * 128 + 128],
                        bet_c[:, kc:kc + 1],
                        start=(kc == 0), stop=(kc == NKC - 1)), s_T,
                        waits=([(s_ldw, t_pc), (s_ldw, t_small), free[7]]
                               if kc == 0 else []))
                tSb = op("scalar", lambda e, pc=pc, m2=m2: e.copy(
                    bxz_c[:, pc * 4 + m2:pc * 4 + m2 + 1], ps[7][:, 0:1]),
                    s_S, waits=[(s_T, tT)])
                free[7] = (s_S, tSb)
            last_sp = dma("gpsimd",
                          win_sc[:, cols].rearrange("(c p) m -> p c m", p=128),
                          stgb[:, 0:NKC * 512].rearrange(
                              "p (c m) -> p c m", m=512), s_sp,
                          waits=[(s_V, tV), (s_S, tSb)])
        t_bxz = tSb

        for (src, dst, nkc) in ((woutT, wout_sc, NIC), (baseT, base_sc, NKC)):
            for pc in range(src.shape[1] // 256):
                c0 = pc * 256
                t_hp = dma("sync",
                           stg[:, 0:nkc * 256].rearrange(
                               "p (c m) -> p c m", m=256),
                           src[:, c0:c0 + 256].rearrange(
                               "(c p) m -> p c m", p=128),
                           s_ldw, waits=[(s_sp, last_sp)])
                tV = op("vector", lambda e, nkc=nkc: e.tensor_copy(
                    stgb[:, 0:nkc * 256], stg[:, 0:nkc * 256]), s_V,
                    waits=[(s_ldw, t_hp)])
                last_sp = dma("gpsimd",
                              dst[:, c0:c0 + 256].rearrange(
                                  "(c p) m -> p c m", p=128),
                              stgb[:, 0:nkc * 256].rearrange(
                                  "p (c m) -> p c m", m=256),
                              s_sp, waits=[(s_V, tV)])

        for i in range(NIC):
            op("vector", lambda e, i=i: e.tensor_reduce(
                wsum_c[:, i:i + 1], cw_c[:, 4 * i:4 * i + 4], AX.X, ALU.add),
                s_V, waits=([(s_ldw, t_small)] if i == 0 else []), drain=True)
        op("vector", lambda e: e.tensor_tensor(
            cb2_c[:], wsum_c[:], bxz_c[:, 0:NIC], ALU.mult), s_V,
            waits=[(s_S, t_bxz)], drain=True)
        op("vector", lambda e: e.tensor_tensor(
            cb2_c[:], cb2_c[:], cb_c[:], ALU.add), s_V)
        for i in range(NIC):
            op("vector", lambda e, i=i: e.tensor_scalar(
                ccar[:, 4 * i:4 * i + 3],
                bxz_c[:, i:i + 1].broadcast_to([128, 3]),
                -1.0, None, ALU.mult), s_V)

        x_t3 = x_hbm.rearrange("(c b p) d -> c b p d", b=TBLK, p=128)
        out_t3 = out_hbm.rearrange("(c b p) d -> c b p d", b=TBLK, p=128)
        sp_xc3 = sp_xc.rearrange("(a p) t -> a p t", p=128)
        sp_sz3 = sp_sz.rearrange("(a p) t -> a p t", p=128)
        sp_g3 = sp_g.rearrange("(a p) t -> a p t", p=128)

        scratch_ready = last_sp
        xT_rd_done = (0, 0)
        xnT_transp = 0
        ln1_srd = 0
        ln1_vrd = 0
        ln2_srd = 0
        ln2_vrd = 0
        xc_free0 = tV_lb
        win_mm = [0, 0, 0, 0]
        wob_mm = [0, 0]
        wx_cons = 0
        ssm_cons = 0
        ssm_free = 0
        ssmT_free = 0
        ymT_done = 0
        y3_free = 0
        xc_free = xc_free0
        dt_free = 0
        gat_free = 0
        tts_hfree = {0: 0, 1: 0, 2: 0, 3: 0}
        carry_hfree = {0: 0, 1: 0, 2: 0, 3: 0}
        ab_free = {0: 0, 1: 0, 2: 0}
        qi_count = 0
        abq_count = 0
        brep_cons = {0: 0, 1: 0, 2: 0}
        tpA_free = 0
        szt_wr = [(s_V, 0), (s_V, 0)]
        gt_wr = [(s_T, 0), (s_T, 0)]
        scr1_freeT = 0
        bc_slot_free = {0: 0, 1: 0, 2: 0}
        midg_rd_prev = 0
        o3_prev = [0, 0]
        lora_mm_prev = 0

        t_x = dma("sync", x_T.ap().rearrange("p (b d) -> p b d", d=DM),
                  x_t3[0].rearrange("b p d -> p b d"), s_ldx)

        for c in range(NCH):
            d_hold = {}
            d_pre = {0: [], 1: [], 2: [], 3: []}
            for dblk in range(4):
                par = dblk % 4
                reg0 = par * NKC * 128
                d_hold[dblk] = dma(
                    "sync" if par % 2 == 0 else "scalar",
                    win_b[:, reg0:reg0 + NKC * 128].rearrange(
                        "p (c m) -> p c m", m=128),
                    win_sc[:, dblk * 128:(dblk + 1) * 128].rearrange(
                        "(c p) m -> p c m", p=128),
                    s_win[par], waits=[(s_T, win_mm[par]),
                                       (s_sp, scratch_ready
                                        if c == 0 else 0)])
            tS_ev = 0
            for tb in range(TBLK):
                xs = x_T[:, tb * DM:(tb + 1) * DM]
                xns = xn_T[:, 0:DM]
                w0 = ([(s_ldx, t_x), (s_T, xnT_transp)] +
                      ([(s_ldw, t_small)] if c == 0 else [])) if tb == 0 else []
                op("vector", lambda e, xs=xs: e.bn_stats(
                    stat[:, 0:6], xs[:, 0:512]), s_V, waits=w0)
                op("vector", lambda e, xs=xs: e.bn_stats(
                    stat[:, 6:12], xs[:, 512:1024]), s_V, drain=True)
                tVa = op("vector", lambda e: e.bn_aggr(
                    stat[:, 12:14], stat[:, 0:12]), s_V,
                    waits=[(s_S, ln1_srd)])
                op("scalar", lambda e: e.activation(
                    stat[:, 14:15], stat[:, 13:14], AF.Ln, bias=LN_EPS), s_S,
                    waits=[(s_V, tVa), (s_V, ln1_vrd)], drain=True)
                op("scalar", lambda e: e.activation(
                    stat[:, 14:15], stat[:, 14:15], AF.Exp, scale=-0.5), s_S,
                    drain=True)
                op("scalar", lambda e: e.activation(
                    stat[:, 15:16], stat[:, 12:13], AF.Copy,
                    scale=stat[:, 14:15]), s_S, drain=True)
                tS4 = op("scalar", lambda e: e.activation(
                    stat[:, 15:16], stat[:, 15:16], AF.Copy, scale=-1.0), s_S)
                ln1_srd = tS4
                tVxn = op("vector", lambda e, xs=xs, xns=xns: e.tensor_scalar(
                    xns, xs, stat[:, 14:15], stat[:, 15:16], ALU.mult,
                    ALU.add), s_V, waits=[(s_S, tS4)])
                ln1_vrd = tVxn
                op("vector", lambda e, xns=xns: e.bn_stats(
                    stat[:, 16:22], xns[:, 0:512]), s_V)
                op("vector", lambda e, xns=xns: e.bn_stats(
                    stat[:, 22:28], xns[:, 512:1024]), s_V, drain=True)
                tVb = op("vector", lambda e: e.bn_aggr(
                    stat[:, 28:30], stat[:, 16:28]), s_V,
                    waits=[(s_S, ln2_srd)])
                op("scalar", lambda e: e.activation(
                    stat[:, 30:31], stat[:, 29:30], AF.Ln, bias=LN_EPS), s_S,
                    waits=[(s_V, tVb), (s_V, ln2_vrd)], drain=True)
                op("scalar", lambda e: e.activation(
                    stat[:, 30:31], stat[:, 30:31], AF.Exp, scale=-0.5), s_S,
                    drain=True)
                op("scalar", lambda e: e.activation(
                    stat[:, 31:32], stat[:, 28:29], AF.Copy,
                    scale=stat[:, 30:31]), s_S, drain=True)
                tS8 = op("scalar", lambda e: e.activation(
                    stat[:, 31:32], stat[:, 31:32], AF.Copy, scale=-1.0), s_S)
                ln2_srd = tS8
                tVnn = op("vector", lambda e, xns=xns: e.tensor_scalar(
                    scr1b[:, 0:DM], xns, stat[:, 30:31], stat[:, 31:32],
                    ALU.mult, ALU.add), s_V, waits=[(s_S, tS8)])
                ln2_vrd = tVnn
                tVxb = op("vector", lambda e, xns=xns: e.tensor_scalar(
                    scr1b[:, DM:2 * DM], xns, 1.0, None, ALU.mult), s_V)
                for (off, dstbuf, wv) in (
                        (0, xnn_f, tVnn), (DM, xn_f, tVxb)):
                    for grp in range(2):
                        pb = 2 + grp
                        tT = None
                        for q4 in range(4):
                            dk = grp * 4 + q4
                            tT = op("tensor",
                                    lambda e, off=off, dk=dk,
                                    pb=pb, q4=q4: e.transpose(
                                        ps[pb].ap().bitcast(BF16)[
                                            :, q4 * 128:(q4 + 1) * 128],
                                        scr1b[:, off + dk * 128:
                                              off + dk * 128 + 128],
                                        idnb[:]), s_T,
                                    waits=([(s_V, wv), (s_ldw, t_small), free[pb]]
                                           if q4 == 0 else []))
                        dst = dstbuf.ap().rearrange(
                            "p (k t) -> p k t", t=TC)[
                            :, grp * 4:grp * 4 + 4, tb * 128:(tb + 1) * 128]
                        tS_ev = op("scalar", lambda e, pb=pb, dst=dst: e.copy(
                            dst, ps[pb].ap().bitcast(BF16)[:, 0:512].rearrange(
                                "p (k t) -> p k t", t=128)), s_S,
                            waits=[(s_T, tT)])
                        free[pb] = (s_S, tS_ev)
                scr1_freeT = tT
                xnT_transp = tT
                xT_rd_done = (tVxn, 0)
            xnf_ok = tS_ev
            if dbg_on:
                dma("gpsimd", dbg_xnf.rearrange("(c p) m -> c p m", p=128)[c],
                    xn_f.ap(), s_sp, waits=[(s_S, xnf_ok)])
                dma("gpsimd", dbg_xnnf.rearrange("(c p) m -> c p m", p=128)[c],
                    xnn_f.ap(), s_sp, waits=[(s_S, xnf_ok)])

            tT = None
            for kc in range(NKC):
                tT = op("tensor", lambda e, kc=kc: e.matmul(
                    ps[0][0:16, 0:TC], rl_r[:, kc * 16:(kc + 1) * 16],
                    xn_f[:, kc * TC:(kc + 1) * TC],
                    start=(kc == 0), stop=(kc == NKC - 1)), s_T,
                    waits=([(s_S, xnf_ok), free[0]] if kc == 0 else []))
            tS_lg = op("scalar", lambda e: e.add(
                lgr[0:16, :], ps[0][0:16, 0:TC], rb_c[0:16, 0:1]), s_S,
                waits=[(s_T, tT)])
            free[0] = (s_S, tS_lg)
            for tb in range(TBLK):
                pb = 2 + tb % 2
                tT = op("tensor", lambda e, tb=tb, pb=pb: e.transpose(
                    ps[pb][:, 0:128], lgr[:, tb * 128:(tb + 1) * 128],
                    idn[:]), s_T, waits=[(s_S, tS_lg), free[pb]])
                tS_ev = op("scalar", lambda e, tb=tb, pb=pb: e.copy(
                    lm_T[:, tb * 16:(tb + 1) * 16], ps[pb][:, 0:16]), s_S,
                    waits=[(s_T, tT)])
                free[pb] = (s_S, tS_ev)
            tV_g = 0
            for tb in range(TBLK):
                o = tb * 12
                lo = tb * 16
                op("vector", lambda e, lo=lo, o=o: e.reduce_max(
                    sfx[:, o + 8:o + 9], lm_T[:, lo:lo + 8], AX.X), s_V,
                    waits=[(s_S, tS_ev)] if tb == 0 else [], drain=True)
                op("vector", lambda e, o=o: e.tensor_scalar(
                    sfx[:, o + 9:o + 10], sfx[:, o + 8:o + 9], -1.0, None,
                    ALU.mult), s_V, drain=True)
                tS_x = op("scalar", lambda e, lo=lo, o=o: e.activation(
                    sfx[:, o:o + 8], lm_T[:, lo:lo + 8], AF.Exp,
                    bias=sfx[:, o + 9:o + 10]), s_S, waits=[(s_V, s_V.n)])
                op("vector", lambda e, o=o: e.tensor_reduce(
                    sfx[:, o + 10:o + 11], sfx[:, o:o + 8], AX.X, ALU.add),
                    s_V, waits=[(s_S, tS_x)], drain=True)
                op("vector", lambda e, o=o: e.reciprocal(
                    sfx[:, o + 11:o + 12], sfx[:, o + 10:o + 11]), s_V,
                    drain=True)
                tV_g = op("vector", lambda e, o=o, tb=tb: e.tensor_scalar(
                    gat_T[:, tb * 8:(tb + 1) * 8], sfx[:, o:o + 8],
                    sfx[:, o + 11:o + 12], None, ALU.mult), s_V)
            tS_gat = 0
            for tb in range(TBLK):
                pb = 2 + tb % 2
                tT = op("tensor", lambda e, tb=tb, pb=pb: e.transpose(
                    ps[pb][0:8, 0:128], gat_T[:, tb * 8:(tb + 1) * 8],
                    idn[:]), s_T, waits=[(s_V, tV_g), free[pb]])
                tS_gat = op("scalar", lambda e, tb=tb, pb=pb: e.copy(
                    gat[:, tb * 128:(tb + 1) * 128], ps[pb][0:8, 0:128]), s_S,
                    waits=[(s_T, tT), (s_rep, gat_free)])
                free[pb] = (s_S, tS_gat)
            t_gr = dma("gpsimd", grep.ap(),
                       gat.ap().unsqueeze(1).broadcast_to([8, 8, TC]),
                       s_rep, waits=[(s_S, tS_gat), (s_V, midg_rd_prev)])
            gat_free = t_gr
            t_mr = t_gr
            for e8 in range(8):
                t_mr = dma("gpsimd", mrep[e8 * 8:(e8 + 1) * 8, 0:TC],
                           lgr[8:16, :], s_rep, waits=[(s_S, tS_lg)])


            xc_sp = {}
            sz_sp = {}
            tV_cc_done = 0
            silu2 = [0, 0]
            wx_xcb = [0, 0]
            wx_pend = None
            for dblk in range(4, NIC):
                d_pre[dblk % 4].append(dblk)
            for dblk in range(NIC):
                par = dblk % 4
                reg0 = par * NKC * 128
                t_wp = d_hold[dblk]
                if True:
                    pbk = dblk % 2
                    tT = None
                    for kc in range(NKC):
                        tT = op("tensor", lambda e, kc=kc, reg0=reg0,
                                pbk=pbk: e.matmul(
                                    ps[pbk][:, 0:TC],
                                    win_b[:, reg0 + kc * 128:
                                          reg0 + kc * 128 + 128],
                                    xnn_f[:, kc * TC:(kc + 1) * TC],
                                    start=(kc == 0), stop=(kc == NKC - 1)),
                                s_T, waits=([(s_win[par], t_wp), free[pbk]]
                                            if kc == 0 else []))
                    win_mm[par] = tT
                    if d_pre[par]:
                        nxt = d_pre[par].pop(0)
                        d_hold[nxt] = dma(
                            "sync" if par % 2 == 0 else "scalar",
                            win_b[:, reg0:reg0 + NKC * 128].rearrange(
                                "p (c m) -> p c m", m=128),
                            win_sc[:, nxt * 128:(nxt + 1) * 128].rearrange(
                                "(c p) m -> p c m", p=128),
                            s_win[par], waits=[(s_T, win_mm[par])])
                    if wx_pend is not None:
                        pi, pcb = wx_pend
                        wx_pend = None
                        wx_cons = op("tensor", lambda e, i=pi: e.matmul(
                            ps[6][0:33, 0:TC], wx_r[:, i * 33:(i + 1) * 33],
                            xcb[:, (i % 2) * TC:(i % 2 + 1) * TC],
                            start=(i == 0), stop=(i == NIC - 1)), s_T,
                            waits=[(s_S, pcb)] +
                            ([free[6]] if pi == 0 else []))
                        wx_xcb[pi % 2] = wx_cons
                    if dblk < NIC:
                        i = dblk
                        pb = ps[pbk]
                        cdst = tp[:, 0:TC] if i % 2 == 0 else scr1[:, 0:TC]
                        tapw = lambda k, i=i: cw_c[:, 4 * i + k:4 * i + k + 1]
                        op("vector", lambda e, pb=pb, i=i, tapw=tapw,
                           cdst=cdst: e.tensor_scalar(
                               cdst, pb[:, 0:TC], tapw(3),
                               cb2_c[:, i:i + 1], ALU.mult, ALU.add), s_V,
                            waits=[(s_T, tT), (s_S, silu2[i % 2])])
                        for k in (2, 1, 0):
                            nb = 3 - k
                            op("vector", lambda e, pb=pb, k=k, nb=nb,
                               tapw=tapw, cdst=cdst:
                                e.scalar_tensor_tensor(
                                    cdst[:, nb:TC], pb[:, 0:TC - nb], tapw(k),
                                    cdst[:, nb:TC], ALU.mult, ALU.add), s_V)
                            op("vector", lambda e, i=i, k=k, nb=nb, tapw=tapw,
                               cdst=cdst:
                                e.scalar_tensor_tensor(
                                    cdst[:, 0:nb], ccar[:, 4 * i + k:4 * i + 3],
                                    tapw(k), cdst[:, 0:nb], ALU.mult, ALU.add),
                                s_V)
                        tV_cc_done = op("vector", lambda e, pb=pb, i=i: e.tensor_copy(
                            ccar[:, 4 * i:4 * i + 3], pb[:, TC - 3:TC]), s_V)
                        free[pbk] = (s_V, tV_cc_done)
                        if i < 8:
                            xdst = xc_h[:, i * TC:(i + 1) * TC]
                            w_extra = [(s_V, xc_free)] if i == 0 else []
                        else:
                            xdst = tp[:, TC:2 * TC]
                            w_extra = [(s_sp, s_sp.n)]
                        tS_xc = op("scalar", lambda e, xdst=xdst, cdst=cdst:
                                   e.activation(xdst, cdst, AF.Silu), s_S,
                                   waits=[(s_V, tV_cc_done)] + w_extra)
                        silu2[i % 2] = tS_xc
                        if i >= 8:
                            tpA_free = dma("gpsimd", sp_xc3[c * 8 + i - 8],
                                           tp[:, TC:2 * TC], s_sp,
                                           waits=[(s_S, tS_xc)])
                            xc_sp[i] = tpA_free
                        tV_cb = op("scalar", lambda e, xdst=xdst, i=i: e.copy(
                            xcb[:, (i % 2) * TC:(i % 2 + 1) * TC], xdst), s_S,
                            waits=[(s_T, wx_xcb[i % 2])])
                        wx_pend = (i, tV_cb)
            if wx_pend is not None:
                pi, pcb = wx_pend
                wx_pend = None
                wx_cons = op("tensor", lambda e, i=pi: e.matmul(
                    ps[6][0:33, 0:TC], wx_r[:, i * 33:(i + 1) * 33],
                    xcb[:, (i % 2) * TC:(i % 2 + 1) * TC],
                    start=(i == 0), stop=(i == NIC - 1)), s_T,
                    waits=[(s_S, pcb)])
                wx_xcb[pi % 2] = wx_cons
            ssm_cons = op("scalar", lambda e: e.copy(
                ssm[0:33, :], ps[6][0:33, 0:TC]), s_S,
                waits=[(s_T, wx_cons), (s_T, ssm_free)])
            tS_smT = 0
            for tb in range(TBLK):
                pb = 2 + tb % 2
                tT = op("tensor", lambda e, tb=tb, pb=pb: e.transpose(
                    ps[pb][:, 0:128], ssm[:, tb * 128:(tb + 1) * 128],
                    idn[:]), s_T,
                    waits=[(s_S, ssm_cons), free[pb], (s_V, ssmT_free)])
                tS_smT = op("scalar", lambda e, tb=tb, pb=pb: e.copy(
                    ssm_T[:, tb * 33:(tb + 1) * 33], ps[pb][:, 0:33]), s_S,
                    waits=[(s_T, tT)])
                free[pb] = (s_S, tS_smT)

            tV_mg = op("vector", lambda e: e.tensor_tensor(
                midg[:], mrep[:, 0:TC], grep[:], ALU.mult), s_V,
                waits=[(s_rep, t_mr),
                       (s_T, max(win_mm))])

            free[6] = (s_S, ssm_cons)

            def mk_z(dblk):
                par = dblk % 4
                reg0 = par * NKC * 128
                hold = {}

                def emit_zd():
                    hold["t"] = dma(
                        "sync" if par % 2 == 0 else "scalar",
                        win_b[:, reg0:reg0 + NKC * 128].rearrange(
                            "p (c m) -> p c m", m=128),
                        win_sc[:, dblk * 128:(dblk + 1) * 128].rearrange(
                            "(c p) m -> p c m", p=128),
                        s_win[par], waits=[(s_T, win_mm[par]),
                                               (s_sp, scratch_ready
                                                if c == 0 else 0)])

                def emit_z():
                    t_wp = hold["t"]
                    pbk = 6 + dblk % 2
                    tT = None
                    for kc in range(NKC):
                        tT = op("tensor", lambda e, kc=kc, reg0=reg0,
                                pbk=pbk: e.matmul(
                                    ps[pbk][:, 0:TC],
                                    win_b[:, reg0 + kc * 128:
                                          reg0 + kc * 128 + 128],
                                    xnn_f[:, kc * TC:(kc + 1) * TC],
                                    start=(kc == 0), stop=(kc == NKC - 1)),
                                s_T, waits=([(s_win[par], t_wp),
                                             free[pbk]]
                                            if kc == 0 else []))
                    win_mm[par] = tT
                    hold["mm"] = tT

                def emit_zf():
                    tT = hold["mm"]
                    pbk = 6 + dblk % 2
                    iz = dblk - NIC
                    tS_z = op("scalar", lambda e, pbk=pbk, iz=iz: e.add(
                        sz_t[:, (iz % 2) * TC:(iz % 2 + 1) * TC],
                        ps[pbk][:, 0:TC],
                        bxz_c[:, NIC + iz:NIC + iz + 1]), s_S,
                        waits=[(s_T, tT), szt_wr[iz % 2]])
                    free[pbk] = (s_S, tS_z)
                    dma("sync", sp_sz3[c * NIC + iz],
                        sz_t[:, (iz % 2) * TC:(iz % 2 + 1) * TC],
                        s_bsp, waits=[(s_S, tS_z)])
                    szt_wr[iz % 2] = (s_bsp, s_bsp.n)
                return emit_zd, emit_z, emit_zf

            def mk_cg(dblk):
                par = dblk % 4
                reg0 = par * NKC * 128
                hold = {}

                def emit_cd():
                    hold["t"] = dma(
                        "sync" if par % 2 == 0 else "scalar",
                        win_b[:, reg0:reg0 + NKC * 128].rearrange(
                            "p (c m) -> p c m", m=128),
                        base_sc[:, dblk * 128:(dblk + 1) * 128].rearrange(
                            "(c p) m -> p c m", p=128),
                        s_win[par], waits=[(s_T, win_mm[par]),
                                               (s_sp, scratch_ready
                                                if c == 0 else 0)])

                def emit_cg():
                    t_bp = hold["t"]
                    pbk = 6 + dblk % 2
                    tT = None
                    for kc in range(NKC):
                        tT = op("tensor", lambda e, kc=kc, reg0=reg0,
                                pbk=pbk: e.matmul(
                                    ps[pbk][:, 0:TC],
                                    win_b[:, reg0 + kc * 128:
                                          reg0 + kc * 128 + 128],
                                    xn_f[:, kc * TC:(kc + 1) * TC],
                                    start=(kc == 0), stop=False), s_T,
                                waits=([(s_win[par], t_bp), free[pbk]]
                                       if kc == 0 else []))
                    tT = op("tensor", lambda e, dblk=dblk, pbk=pbk: e.matmul(
                        ps[pbk][:, 0:TC], lb_r[:, dblk * 128:(dblk + 1) * 128],
                        midg[:], start=False, stop=True), s_T,
                        waits=[(s_V, tV_mg)])
                    win_mm[par] = tT
                    hold["mm"] = tT

                def emit_cf():
                    tT = hold["mm"]
                    pbk = 6 + dblk % 2
                    tS_g = op("scalar", lambda e, dblk=dblk, pbk=pbk:
                              e.add(
                                  g_t[:, (dblk % 2) * TC:(dblk % 2 + 1) * TC],
                                  ps[pbk][:, 0:TC],
                                  bb_c[:, dblk:dblk + 1]), s_S,
                              waits=[(s_T, tT), gt_wr[dblk % 2]])
                    free[pbk] = (s_S, tS_g)
                    dma("sync", sp_g3[c * NKC + dblk],
                        g_t[:, (dblk % 2) * TC:(dblk % 2 + 1) * TC],
                        s_bsp, waits=[(s_S, tS_g)])
                    gt_wr[dblk % 2] = (s_bsp, s_bsp.n)
                return emit_cd, emit_cg, emit_cf

            bg_pairs = ([mk_z(d) for d in range(NIC, 32)] +
                        [mk_cg(d) for d in range(NKC)])
            bg_dmas = [p[0] for p in bg_pairs]
            bg_queue = [p[1] for p in bg_pairs]
            bg_fins = [p[2] for p in bg_pairs]
            bg_nmm = 0
            bg_nfin = 0
            for _ in range(3):
                bg_dmas.pop(0)()
            midg_rd_prev = tV_mg

            ssm_lastT = 0
            for half in range(2):
                if half == 1:
                    t_xcl = 0
                    for j in range(8):
                        t_xcl = dma("sync", xc_h[:, j * TC:(j + 1) * TC],
                                    sp_xc3[c * 8 + j], s_xcl,
                                    waits=[(s_sp, s_sp.n),
                                           (s_V, xc_free)])
                tT_bc = op("tensor", lambda e: e.matmul(
                    ps[4][:, 0:TC], onesr[0:1, 0:128], ssm[0:1, 0:TC],
                    start=True, stop=True), s_T,
                    waits=[(s_S, ssm_cons), (s_ldw, t_small), free[4]])
                ssm_lastT = tT_bc
                tS_dt = 0
                tV_u = 0
                u_j = [0] * 8
                for j in range(8):
                    i = half * 8 + j
                    op("scalar", lambda e, i=i: e.activation(
                        e1t[:], ps[4][:, 0:TC], AF.Exp,
                        bias=bdt_c[:, i:i + 1], scale=wdt_c[:, i:i + 1]), s_S,
                        waits=[(s_T, tT_bc)] +
                        ([(s_V, dt_free), (s_T, scr1_freeT)]
                         if j == 0 else []) +
                        ([(s_sp, scratch_ready)]
                         if (c == 0 and half == 0 and j == 0) else []))
                    tS_dt = op("scalar", lambda e, j=j: e.activation(
                        dt_h[:, j * TC:(j + 1) * TC], e1t[:], AF.Ln,
                        bias=1.0), s_S)
                    w_u = [(s_S, tS_dt)]
                    if half == 1 and j == 0:
                        w_u.append((s_xcl, t_xcl))
                    if c == 0 and half == 0 and j == 0:
                        w_u.append((s_sp, scratch_ready))
                    tV_u = op("vector", lambda e, j=j: e.tensor_tensor(
                        u_h[:, j * TC:(j + 1) * TC],
                        dt_h[:, j * TC:(j + 1) * TC],
                        xc_h[:, j * TC:(j + 1) * TC], ALU.mult), s_V,
                        waits=w_u)
                    u_j[j] = tV_u
                free[4] = (s_S, tS_dt)
                dt_free = tV_u

                ymT_prev = ymT_done
                def emit_bbc(n):
                    bbk = 4 + n % 2
                    slot = 32 * (n % 3)
                    t_brow = dma("gpsimd", bc_st[slot:slot + 1, :],
                                 ssm[1 + n:2 + n, 0:TC], s_bc[n % 3],
                                 waits=[(s_S, ssm_cons),
                                        (s_T, bc_slot_free[n % 3])])
                    tT_bb = op("tensor", lambda e, n=n, bbk=bbk, slot=slot:
                               e.matmul(
                                   ps[bbk][:, 0:TC],
                                   onesr[slot:slot + 1, 0:128],
                                   bc_st[slot:slot + 1, 0:TC],
                                   start=True, stop=True), s_T,
                               waits=[(s_bc[n % 3], t_brow), free[bbk]])
                    bc_slot_free[n % 3] = tT_bb
                    tS_bev = op("scalar", lambda e, n=n, bbk=bbk: e.copy(
                        b_rep[:, (n % 3) * TC:(n % 3 + 1) * TC],
                        ps[bbk][:, 0:TC]), s_S,
                        waits=[(s_T, tT_bb), (s_V, brep_cons[n % 3])])
                    free[bbk] = (s_S, tS_bev)
                    return (tT_bb, tS_bev)

                def emit_ym(n, qps, tts_q):
                    lastT = {0: None, 1: None}
                    tV_ym = 0
                    for tb in range(TBLK):
                        pb = (2, 3, 0, 1)[tb]
                        tT_h = None
                        for q in range(2):
                            qp = qps[q]
                            for jj in range(4):
                                tT_h = op("tensor", lambda e, qp=qp, jj=jj,
                                          tb=tb, pb=pb, q=q: e.transpose(
                                              ps[pb].ap().bitcast(BF16)[
                                                  :, q * 512 + jj * 128:
                                                  q * 512 + jj * 128 + 128],
                                              h_q[:, qp * QW + jj * 513 + 1 +
                                                  tb * 128:qp * QW + jj * 513
                                                  + 1 + tb * 128 + 128],
                                              idnb[:]), s_T,
                                          waits=([(s_V, tts_q[q]), free[pb]]
                                                 if jj == 0 else []))
                            lastT[q] = tT_h
                        ysl = tb * 1024
                        if n == 0:
                            tV_ym = op("vector", lambda e, ysl=ysl,
                                       pb=pb, tb=tb, n=n: e.tensor_scalar(
                                           y_Th[:, ysl:ysl + 1024],
                                           ps[pb].ap().bitcast(BF16)[:, 0:1024],
                                           ssm_T[:, tb * 33 + 17 + n:
                                                 tb * 33 + 18 + n],
                                           None, ALU.mult), s_V,
                                       waits=[(s_T, tT_h),
                                              (s_S, tS_smT),
                                              (s_T, ymT_prev)])
                        else:
                            tV_ym = op("vector", lambda e, ysl=ysl,
                                       pb=pb, tb=tb, n=n:
                                       e.scalar_tensor_tensor(
                                           y_Th[:, ysl:ysl + 1024],
                                           ps[pb].ap().bitcast(BF16)[:, 0:1024],
                                           ssm_T[:, tb * 33 + 17 + n:
                                                 tb * 33 + 18 + n],
                                           y_Th[:, ysl:ysl + 1024],
                                           ALU.mult, ALU.add), s_V,
                                       waits=[(s_T, tT_h)])
                        free[pb] = (s_V, tV_ym)
                    tts_hfree[qps[0]] = lastT[0]
                    tts_hfree[qps[1]] = lastT[1]
                    return tV_ym

                pend = None
                tT_bb_pre = emit_bbc(0)
                for n in range(NST):
                    bbk = 4 + n % 2
                    tT_bb, tS_bev = tT_bb_pre
                    ssm_lastT = tT_bb
                    if n + 1 < NST:
                        tT_bb_pre = emit_bbc(n + 1)
                    tts_q = [0, 0]
                    qps = [0, 0]
                    for q in range(2):
                        parq = abq_count % 2
                        para = abq_count % 3
                        qp = qi_count % 4
                        qps[q] = qp
                        abq_count += 1
                        qi_count += 1
                        tS_ab = 0
                        for jj in range(4):
                            j = q * 4 + jj
                            i = half * 8 + j
                            tS_ab = op("scalar", lambda e, j=j, i=i, n=n,
                                       para=para, jj=jj: e.activation(
                                           abar[:, para * QW + jj * 513 + 1:
                                                para * QW + jj * 513 + 513],
                                           dt_h[:, j * TC:(j + 1) * TC],
                                           AF.Exp,
                                           scale=a_c[:, i * NST + n:
                                                     i * NST + n + 1]),
                                       s_S, waits=([(s_V, ab_free[para])]
                                                   if jj == 0 else []))
                        bdst = beta.ap().rearrange(
                            "p (x s) -> p x s", s=513)[
                            :, 4 * parq:4 * parq + 4, 1:513]
                        bsrc_u = u_h.ap().rearrange(
                            "p (x t) -> p x t", t=TC)[
                            :, q * 4:q * 4 + 4, :]
                        bsrc_b = b_rep.ap()[
                            :, (n % 3) * TC:(n % 3 + 1) * TC].unsqueeze(
                            1).broadcast_to([128, 4, TC])
                        tV_be = op("vector", lambda e, bdst=bdst,
                                   bsrc_u=bsrc_u, bsrc_b=bsrc_b:
                                   e.tensor_tensor(bdst, bsrc_u, bsrc_b,
                                                   ALU.mult), s_V,
                                   waits=[(s_S, tS_bev)])
                        csrc = carry.ap().rearrange(
                            "p (i n2) -> p i n2", n2=NST)[
                            :, half * 8 + q * 4:half * 8 + q * 4 + 4, n]
                        bspc = beta.ap().rearrange(
                            "p (x s) -> p x s", s=513)[
                            :, 4 * parq:4 * parq + 4, 0]
                        tS_spc = op("scalar", lambda e, bspc=bspc, csrc=csrc:
                                    e.copy(bspc, csrc), s_S)
                        tts_q[q] = op("vector", lambda e, parq=parq, qp=qp,
                                      para=para:
                                      e.tensor_tensor_scan(
                                          h_q[:, qp * QW:(qp + 1) * QW],
                                          abar[:, para * QW:(para + 1) * QW],
                                          beta[:, parq * QW:(parq + 1) * QW],
                                          0.0, ALU.mult, ALU.add), s_V,
                                      waits=[(s_S, tS_ab), (s_S, tS_spc),
                                             (s_T, tts_hfree[qp]),
                                             (s_S, carry_hfree[qp])])
                        ab_free[para] = tts_q[q]
                    qp0 = qps[0]
                    csr2 = h_q.ap()[
                        :, qp0 * QW:(qp0 + 2) * QW].rearrange(
                        "p (x s) -> p x s", s=513)[:, :, 512]
                    cdst = carry.ap().rearrange(
                        "p (i n2) -> p i n2", n2=NST)[
                        :, half * 8:half * 8 + 8, n]
                    carry_sval = op("scalar", lambda e, csr2=csr2,
                                    cdst=cdst: e.copy(cdst, csr2), s_S,
                                    waits=[(s_V, tts_q[1])])
                    carry_hfree[qps[0]] = carry_sval
                    carry_hfree[qps[1]] = carry_sval
                    brep_cons[n % 3] = tts_q[1]
                    if bg_dmas:
                        bg_dmas.pop(0)()
                    if bg_queue:
                        bg_queue.pop(0)()
                        bg_nmm += 1
                    if bg_nmm > bg_nfin:
                        bg_fins.pop(0)()
                        bg_nfin += 1
                    if pend is not None:
                        tV_ym = emit_ym(*pend)
                    pend = (n, qps, tts_q)
                tV_ym = emit_ym(*pend)
                pend = None
                ssmT_free = tV_ym
                if half == 1:
                    w_hold = {}
                    for dblk in range(2):
                        par = dblk % 2
                        reg0 = par * NIC * 128
                        w_hold[dblk] = dma(
                            "scalar",
                            wob_b[:, reg0:reg0 + NIC * 128].rearrange(
                                "p (c m) -> p c m", m=128),
                            wout_sc[:, dblk * 128:(dblk + 1) * 128].rearrange(
                                "(c p) m -> p c m", p=128),
                            s_wob[par], waits=[(s_T, wob_mm[par]),
                                               (s_sp, scratch_ready
                                                if c == 0 else 0)])

                tV_y3 = 0
                for j in range(8):
                    i = half * 8 + j
                    pb = 2 + j % 2
                    tT_y = None
                    for tb in range(TBLK):
                        tT_y = op("tensor", lambda e, j=j, tb=tb, pb=pb:
                                  e.transpose(
                                      ps[pb][:, tb * 128:(tb + 1) * 128],
                                      y_Th[:, tb * 1024 + j * 128:
                                           tb * 1024 + j * 128 + 128],
                                      idn[:]), s_T,
                                  waits=([(s_V, tV_ym), free[pb]]
                                         if tb == 0 else []))
                    ymT_done = tT_y
                    t_szl = dma("sync", sz_t[:, (j % 2) * TC:(j % 2 + 1) * TC],
                                sp_sz3[c * NIC + i], s_szl[j % 2],
                                waits=[(s_bsp, s_bsp.n), szt_wr[j % 2]])
                    tS_sl = op("scalar", lambda e, j=j: e.activation(
                        sz_t[:, (j % 2) * TC:(j % 2 + 1) * TC],
                        sz_t[:, (j % 2) * TC:(j % 2 + 1) * TC], AF.Silu),
                        s_S, waits=[(s_szl[j % 2], t_szl)])
                    tV_y2 = op("vector", lambda e, j=j, i=i, pb=pb:
                               e.scalar_tensor_tensor(
                                   tp2[:], xc_h[:, j * TC:(j + 1) * TC],
                                   d_c[:, i:i + 1], ps[pb][:, 0:TC],
                                   ALU.mult, ALU.add), s_V,
                               waits=[(s_T, tT_y)])
                    free[pb] = (s_V, tV_y2)
                    tV_y3 = op("vector", lambda e, j=j, i=i: e.tensor_tensor(
                        y3[:, i * TC:(i + 1) * TC], tp2[:],
                        sz_t[:, (j % 2) * TC:(j % 2 + 1) * TC], ALU.mult),
                        s_V, waits=[(s_S, tS_sl)] +
                        ([(s_T, y3_free)] if (half == 0 and j == 0) else []))
                    szt_wr[j % 2] = (s_V, tV_y3)
                xc_free = tV_y3

            if dbg_on:
                dma("gpsimd", dbg_y3.rearrange("(c p) m -> c p m", p=128)[c],
                    y3.ap(), s_sp, waits=[(s_V, xc_free)])

            while bg_dmas:
                bg_dmas.pop(0)()
            while bg_queue:
                bg_queue.pop(0)()
            while bg_fins:
                bg_fins.pop(0)()
            for dblk in range(NKC):
                par = dblk % 2
                reg0 = par * NIC * 128
                t_wo = w_hold[dblk]
                pbk = 2 + dblk % 2
                tT = None
                for ki in range(NIC):
                    tT = op("tensor", lambda e, ki=ki, reg0=reg0, pbk=pbk:
                            e.matmul(
                                ps[pbk][:, 0:TC],
                                wob_b[:, reg0 + ki * 128:reg0 + ki * 128 + 128],
                                y3[:, ki * TC:(ki + 1) * TC],
                                start=(ki == 0), stop=False), s_T,
                            waits=([(s_wob[par], t_wo), (s_V, xc_free),
                                    free[pbk]] if ki == 0 else []))
                tT = op("tensor", lambda e, pbk=pbk, dblk=dblk: e.matmul(
                    ps[pbk][:, 0:TC], idnb[:],
                    xn_f[:, dblk * TC:(dblk + 1) * TC],
                    start=False, stop=True), s_T)
                wob_mm[par] = tT
                y3_free = tT
                if dblk + 2 < NKC:
                    w_hold[dblk + 2] = dma(
                        "scalar",
                        wob_b[:, reg0:reg0 + NIC * 128].rearrange(
                            "p (c m) -> p c m", m=128),
                        wout_sc[:, (dblk + 2) * 128:(dblk + 3) * 128].rearrange(
                            "(c p) m -> p c m", p=128),
                        s_wob[par], waits=[(s_T, tT)])
                tS_s = op("scalar", lambda e, pbk=pbk, par=par: e.copy(
                    sz_t[:, par * TC:(par + 1) * TC],
                    ps[pbk][:, 0:TC]), s_S,
                    waits=[(s_T, tT), szt_wr[par]])
                free[pbk] = (s_S, tS_s)
                t_gl = dma("sync", g_t[:, par * TC:(par + 1) * TC],
                           sp_g3[c * NKC + dblk], s_gl[par],
                           waits=[(s_bsp, s_bsp.n), gt_wr[par]])
                tS_gsl = op("scalar", lambda e, par=par: e.activation(
                    g_t[:, par * TC:(par + 1) * TC],
                    g_t[:, par * TC:(par + 1) * TC], AF.Silu), s_S,
                    waits=[(s_gl[par], t_gl)])
                t_xr = dma("sync",
                           x_rt[:, par * TC:(par + 1) * TC].rearrange(
                               "p (b d) -> p b d", d=128),
                           x_t3[c].rearrange("b p d -> p b d")[
                               :, :, dblk * 128:(dblk + 1) * 128],
                           s_xrt[par], waits=[(s_V, o3_prev[par])])
                pbs = 4 + dblk % 2
                pbg = 4 + (dblk + 1) % 2
                tT_s = None
                for tb in range(TBLK):
                    tT_s = op("tensor", lambda e, tb=tb, par=par, pbs=pbs:
                              e.transpose(
                                  ps[pbs][:, tb * 128:(tb + 1) * 128],
                                  sz_t[:, par * TC + tb * 128:
                                       par * TC + tb * 128 + 128],
                                  idn[:]), s_T,
                              waits=([(s_S, tS_s), free[pbs]]
                                     if tb == 0 else []))
                szt_wr[par] = (s_T, tT_s)
                tT_g = None
                for tb in range(TBLK):
                    tT_g = op("tensor", lambda e, tb=tb, par=par, pbg=pbg:
                              e.transpose(
                                  ps[pbg][:, tb * 128:(tb + 1) * 128],
                                  g_t[:, par * TC + tb * 128:
                                      par * TC + tb * 128 + 128],
                                  idn[:]), s_T,
                              waits=([(s_S, tS_gsl), free[pbg]]
                                     if tb == 0 else []))
                gt_wr[par] = (s_T, tT_g)
                osl = ostage[:, par * TC:(par + 1) * TC].rearrange(
                    "p (b d) -> p b d", d=128)
                xrl = x_rt[:, par * TC:(par + 1) * TC].rearrange(
                    "p (b d) -> p b d", d=128)
                ps_s = ps[pbs].ap().rearrange("p (b d) -> p b d", d=128)
                gT = tp[:, par * TC:(par + 1) * TC]
                tS_g2 = op("scalar", lambda e, gT=gT, pbg=pbg: e.copy(
                    gT, ps[pbg][:, 0:TC]), s_S, waits=[(s_T, tT_g)])
                free[pbg] = (s_S, tS_g2)
                op("vector", lambda e, osl=osl, ps_s=ps_s, gT=gT:
                   e.tensor_tensor(osl, ps_s,
                                   gT.rearrange("p (b d) -> p b d", d=128),
                                   ALU.mult), s_V,
                   waits=[(s_T, tT_s), (s_S, tS_g2),
                          (s_ost[par], s_ost[par].n)])
                free[pbs] = (s_V, s_V.n)
                tV_o = op("vector", lambda e, osl=osl, xrl=xrl:
                          e.tensor_tensor(osl, osl, xrl, ALU.add), s_V,
                          waits=[(s_xrt[par], t_xr)])
                o3_prev[par] = tV_o
                dma("gpsimd",
                    out_t3[c].rearrange("b p d -> p b d")[
                        :, :, dblk * 128:(dblk + 1) * 128],
                    ostage[:, par * TC:(par + 1) * TC].rearrange(
                        "p (b d) -> p b d", d=128),
                    s_ost[par], waits=[(s_V, tV_o)])
                gt_wr[par] = (s_ost[par], s_ost[par].n)
            ssm_free = ssm_lastT
            if c + 1 < NCH:
                t_x = dma("sync",
                          x_T.ap().rearrange("p (b d) -> p b d", d=DM),
                          x_t3[c + 1].rearrange("b p d -> p b d"),
                          s_ldx, waits=[(s_V, xT_rd_done[0]),
                                        (s_S, xT_rd_done[1])])

        B.wait("sync", s_ost[0], s_ost[0].n)
        B.wait("sync", s_ost[1], s_ost[1].n)

        stk.enter_context(nc.allow_non_contiguous_dma(
            reason="small one-time vector layout loads"))
        for eng_name in Builder.ENGINES:
            deco = getattr(block, eng_name)

            def make(fns):
                def body(engine):
                    for f in fns:
                        f(engine)
                return body
            deco(make(B.ops[eng_name]))

    return nc


_PROG_CACHE = {}


def _get_prog(T):
    if T not in _PROG_CACHE:
        _PROG_CACHE[T] = build_program(T)
    return _PROG_CACHE[T]


def kernel(**inputs):
    x = np.asarray(inputs["L0_attn_x"])
    B_, T, _ = x.shape
    asc = np.ascontiguousarray
    w = {
        "winT": asc(np.asarray(inputs["L0_attn_W_in"]).T),
        "woutT": asc(np.asarray(inputs["L0_attn_W_out"]).T),
        "baseT": asc(np.asarray(inputs["L0_ffn_base_w"]).T),
        "wxT": asc(np.asarray(inputs["L0_attn_W_x"]).T),
        "loraB": asc(np.asarray(inputs["L0_ffn_lora_B"]).reshape(64, DM)),
        "rlT": asc(np.concatenate(
            [np.asarray(inputs["L0_ffn_router_w"]).T,
             np.asarray(inputs["L0_ffn_lora_A"])], axis=1)),
        "ln_g": asc(np.asarray(inputs["L0_attn_ln_gamma"])),
        "ln_b": asc(np.asarray(inputs["L0_attn_ln_beta"])),
        "conv_w": asc(np.asarray(inputs["L0_attn_conv_w"])),
        "conv_b": asc(np.asarray(inputs["L0_attn_conv_b"])),
        "w_dt": asc(np.asarray(inputs["L0_attn_w_dt"])),
        "b_dt": asc(np.asarray(inputs["L0_attn_b_dt"])),
        "a_log": asc(np.asarray(inputs["L0_attn_A_log"])),
        "d_prm": asc(np.asarray(inputs["L0_attn_D_param"])),
        "base_b": asc(np.asarray(inputs["L0_ffn_base_b"])),
        "rout_b": np.concatenate(
            [np.asarray(inputs["L0_ffn_router_b"], np.float32).reshape(8, 1),
             np.zeros((8, 1), np.float32)]),
        "ident": np.eye(128, dtype=np.float32),
        "identb": np.eye(128, dtype=ml_dtypes.bfloat16),
        "ones1": np.ones((96, 128), dtype=np.float32),
        "cvals": np.tile(np.array([[1e-5, -0.5, -1.0, 1.0]], np.float32),
                         (128, 1)),
    }
    nc = _get_prog(T)
    in_maps = [dict(w, x=asc(x[b])) for b in range(B_)]
    res = run_bass_kernel_spmd(nc, in_maps, core_ids=list(range(B_)))
    return np.stack([np.asarray(r["out"]) for r in res.results], axis=0)
